# version 14
# baseline (speedup 1.0000x reference)
import math
import numpy as np
import ml_dtypes
from contextlib import ExitStack

import concourse.bass as bass
import concourse.mybir as mybir
from concourse.bass_utils import run_bass_kernel_spmd

F32 = mybir.dt.float32
BF16 = mybir.dt.bfloat16
AF = mybir.ActivationFunctionType
ALU = mybir.AluOpType
AX = mybir.AxisListType.X
BF = ml_dtypes.bfloat16

NCORES = 8
H = 32
ROPE = 64
NOPE = 64
VD = 128
QHD = 128
KVR = 256
D = 4096
DQA = 1024
NE = 16
INTER = 448
DI = NE * INTER
KCH = D // 128
NIC = DI // 128
EPS_RMS = 1e-6
EPS_LN = 1e-5
_m = 0.1 * 1.0 * math.log(128.0) + 1.0
SCALE = float(QHD ** (-0.5) * _m * _m)
NEG = -3.0e38


def yarn_cos_sin(seq_len, dim=ROPE, base=10000.0, factor=128.0, beta_fast=32.0,
                 beta_slow=1.0, orig_max=8192, mscale=1.0, mscale_all_dim=1.0):
    def corr_dim(r):
        return dim * math.log(orig_max / (r * 2 * math.pi)) / (2 * math.log(base))
    low = max(math.floor(corr_dim(beta_fast)), 0)
    high = min(math.ceil(corr_dim(beta_slow)), dim - 1)
    hi = high + 0.001 if low == high else float(high)
    ramp = np.clip((np.arange(dim // 2, dtype=np.float32) - low) / (hi - low), 0.0, 1.0)
    inv_freq_mask = 1.0 - ramp
    freq_extra = 1.0 / base ** (np.arange(0, dim, 2, dtype=np.float32) / dim)
    freq_inter = freq_extra / factor
    inv_freq = freq_inter * (1.0 - inv_freq_mask) + freq_extra * inv_freq_mask
    t = np.arange(seq_len, dtype=np.float32)
    freqs = np.outer(t, inv_freq)
    emb = np.concatenate([freqs, freqs], axis=-1)
    def gm(s, m):
        return 1.0 if s <= 1 else 0.1 * m * math.log(s) + 1.0
    ms = gm(factor, mscale) / gm(factor, mscale_all_dim)
    return (np.cos(emb) * ms).astype(np.float32), (np.sin(emb) * ms).astype(np.float32)


def rope_perms():
    d1 = np.zeros((ROPE, ROPE), np.float32)
    d2 = np.zeros((ROPE, ROPE), np.float32)
    for i in range(ROPE // 2):
        d1[i, 2 * i] = 1.0
        d1[ROPE // 2 + i, 2 * i + 1] = 1.0
        d2[i, 2 * i + 1] = -1.0
        d2[ROPE // 2 + i, 2 * i] = 1.0
    return d1, d2


class Ms:
    __slots__ = ("eng", "idx", "dma", "queue")
    def __init__(self, eng, idx, dma=False):
        self.eng = eng
        self.idx = idx
        self.dma = dma


class Op:
    __slots__ = ("emit", "deps", "inc_needed", "dma", "val", "semname")
    def __init__(self, emit, dma):
        self.emit = emit
        self.deps = []
        self.inc_needed = False
        self.dma = dma
        self.val = None


class Builder:
    def __init__(self, nc, mk_sem):
        self.nc = nc
        self.mk_sem = mk_sem
        self.ops = {e: [] for e in ("t", "v", "s", "g", "q")}
        self.dma_ops = []
        self.dma_count = {}
        self.dma_last = {}
        self.lastw = {}
        self.lastr = {}
        self.last_cms = {}
        self.bar_n = 0

    def _deps_for(self, reads, writes):
        deps = []
        for k in reads:
            for m in self.lastw.get(k, {}).values():
                deps.append(m)
        for k in writes:
            for m in self.lastw.get(k, {}).values():
                deps.append(m)
            for m in self.lastr.get(k, {}).values():
                deps.append(m)
        return deps

    def _update_rw(self, key_ms, reads, writes):
        for k in writes:
            self.lastw.setdefault(k, {})[key_ms.eng if not key_ms.dma else ("d", key_ms.eng)] = key_ms
            self.lastr[k] = {}
        for k in reads:
            self.lastr.setdefault(k, {})[key_ms.eng if not key_ms.dma else ("dr", key_ms.eng, key_ms.idx)] = key_ms

    def op(self, eng, emit, reads=(), writes=(), extra_deps=()):
        reads = tuple(reads)
        writes = tuple(writes)
        o = Op(emit, False)
        deps = list(extra_deps) + self._deps_for(reads, writes)
        ms = Ms(eng, len(self.ops[eng]), False)
        self._update_rw(ms, reads, writes)
        o.deps = [d for d in deps
                  if not (eng == "t" and not d.dma and d.eng == "t")]
        self.ops[eng].append(o)
        self.last_cms[eng] = ms
        return ms

    def dma(self, queue, out_ap, in_ap, reads=(), writes=(), sem="cst", **kw):
        sem = f"{queue}_{sem}"
        reads = tuple(reads)
        writes = tuple(writes)
        def emit(eng, _o=out_ap, _i=in_ap, _kw=kw):
            return eng.dma_start(out=_o, in_=_i, **_kw)
        o = Op(emit, True)
        o.inc_needed = True
        deps = self._deps_for(reads, writes)
        prev = self.dma_last.get(sem)
        if prev is not None:
            deps.append(prev)
        cnt = self.dma_count.get(sem, 0) + 16
        self.dma_count[sem] = cnt
        ms = Ms(sem, cnt, True)
        ms.queue = queue
        o.val = cnt
        o.deps = deps
        self._update_rw(ms, reads, writes)
        self.dma_last[sem] = ms
        self.ops[queue].append(o)
        self.dma_ops.append((sem, o, queue))
        return ms

    def barrier(self):
        self.bar_n += 1
        wkeys = []
        for e in ("t", "v", "s", "g"):
            k = ("__barw__", self.bar_n, e)
            extra = [self.last_cms[e]] if e in self.last_cms else []
            self.op(e, lambda eng: None, reads=(), writes=(k,), extra_deps=extra)
            wkeys.append(k)
        extra = list(self.dma_last.values())
        for e in ("t", "v", "s", "g", "q"):
            self.op(e, lambda eng: None, reads=tuple(wkeys), writes=(),
                    extra_deps=extra)

    def finalize_and_emit(self, block, sems):
        for elist in self.ops.values():
            for o in elist:
                for d in o.deps:
                    if not d.dma:
                        self.ops[d.eng][d.idx].inc_needed = True
        for e, elist in self.ops.items():
            cc = 0
            for o in elist:
                if o.dma:
                    continue
                if o.inc_needed:
                    cc += 1
                o.val = cc
        dsem_handles = {name: self.mk_sem(f"dm_{name}")
                        for name in self.dma_count}
        bld = self

        def emit_for(engkey, engobj):
            hwm = {}
            for o in bld.ops[engkey]:
                need = {}
                for d in o.deps:
                    if d.dma:
                        skey = ("dma", d.eng)
                        v = d.idx
                    else:
                        dep = bld.ops[d.eng][d.idx]
                        if not dep.inc_needed or dep.val == 0:
                            continue
                        skey = ("c", d.eng)
                        v = dep.val
                    if need.get(skey, -1) < v:
                        need[skey] = v
                for skey in sorted(need):
                    v = need[skey]
                    if hwm.get(skey, 0) >= v:
                        continue
                    sem = dsem_handles[skey[1]] if skey[0] == "dma" else sems[skey[1]]
                    engobj.wait_ge(sem, v)
                    hwm[skey] = v
                inst = o.emit(engobj)
                if o.inc_needed:
                    if o.dma:
                        inst.then_inc(dsem_handles[o.semname], 16)
                    else:
                        if inst is None:
                            skey = ("c", engkey)
                            pv = o.val - 1
                            if pv > 0 and hwm.get(skey, 0) < pv:
                                engobj.wait_ge(sems[engkey], pv)
                                hwm[skey] = pv
                            inst = engobj.nop()
                        inst.then_inc(sems[engkey], 1)

        for sem, o, queue in self.dma_ops:
            o.semname = sem

        @block.sync
        def _(sync):
            emit_for("q", sync)
            for name, tot in sorted(bld.dma_count.items()):
                sync.wait_ge(dsem_handles[name], tot)

        @block.tensor
        def _(tensor):
            emit_for("t", tensor)

        @block.vector
        def _(vector):
            emit_for("v", vector)

        @block.scalar
        def _(scalar):
            emit_for("s", scalar)

        @block.gpsimd
        def _(gp):
            emit_for("g", gp)


def build_program(S):
    T = S // NCORES
    NSUB = T // 64
    NP = NSUB // 2
    NKT = S // 512
    TT = T // 512
    assert NSUB * 64 == T and NP * 2 == NSUB and NKT * 512 == S and TT * 512 == T

    nc = bass.Bass("TRN2", target_bir_lowering=False)

    def din(name, shape, dt=F32):
        return nc.dram_tensor(name, list(shape), dt, kind="ExternalInput").ap()

    xT = din("xT", (D, S))
    xqT = din("xqT", (D, T))
    wqaT = din("wqaT", (DQA // 128 * 128, D))
    wqbT = din("wqbT", (DQA // 128, 128, D))
    wkvaT = din("wkvaT", (KCH, 128, KVR + ROPE))
    wkvbT = din("wkvbT", (KVR // 128, 128, H * (NOPE + VD)))
    woTs = din("woTs", (KCH * 128, D))
    uTs = din("uTs", (NIC * 128, D))
    gTs = din("gTs", (NIC * 128, D))
    dTs = din("dTs", (KCH * 128, DI))
    gwT = din("gwT", (KCH, 128, NE))
    qlnw = din("qlnw", (128, DQA // 128))
    kvlnw = din("kvlnw", (128, KVR // 128))
    coskT = din("coskT", (ROPE, S))
    sinkT = din("sinkT", (ROPE, S))
    cosqT = din("cosqT", (ROPE, T))
    sinqT = din("sinqT", (ROPE, T))
    maskA = din("maskA", (128, 512), BF16)
    maskB = din("maskB", (128, 512), BF16)

    outT = nc.dram_tensor("outT", [D, T], F32, kind="ExternalOutput").ap()

    attT_sc = nc.dram_tensor("attT_sc", [D, T], BF16).ap()

    id128b_c = nc.inline_tensor(np.eye(128, dtype=BF), name="id128b").ap()
    id128f_c = nc.inline_tensor(np.eye(128, dtype=np.float32), name="id128f").ap()
    id16f_c = nc.inline_tensor(np.eye(16, dtype=np.float32), name="id16f").ap()
    d1np, d2np = rope_perms()
    d1T_c = nc.inline_tensor(np.ascontiguousarray(d1np.T).astype(BF), name="d1T").ap()
    d2T_c = nc.inline_tensor(np.ascontiguousarray(d2np.T).astype(BF), name="d2T").ap()
    ones128_c = nc.inline_tensor(np.ones((128, 1), dtype=BF), name="ones128").ap()
    onesk1_c = nc.inline_tensor(np.ones((1, 128), dtype=BF), name="onesk1").ap()
    esel_np = np.zeros((NE, DI), dtype=np.float32)
    for e in range(NE):
        esel_np[e, e * INTER:(e + 1) * INTER] = 1.0
    esel_c = nc.inline_tensor(esel_np.astype(BF), name="esel").ap()

    with ExitStack() as es:
        sems = {e: es.enter_context(nc.semaphore(f"p_{e}")) for e in
                ("t", "v", "s", "g", "q")}
        _semcache = {}

        def mk_sem(name):
            if name not in _semcache:
                _semcache[name] = es.enter_context(nc.semaphore(name))
            return _semcache[name]

        b = Builder(nc, mk_sem)

        _uniq = [0]

        import os as _os
        def sb(stack, name, shape, dt):
            _uniq[0] += 1
            t = stack.enter_context(
                nc.sbuf_tensor(f"{name}_{_uniq[0]}", list(shape), dt))
            if _os.environ.get("SBUF_DEBUG"):
                print(f"alloc {name:12s} {str(shape):24s} remaining={nc.sbuf_bytes_remaining}")
            return t

        def psb(stack, name, shape, dt=F32):
            _uniq[0] += 1
            return stack.enter_context(
                nc.psum_tensor(f"{name}_{_uniq[0]}", list(shape), dt))

        eps_t = sb(es, "eps_t", (1, 2), F32)
        b.op("g", lambda eng: eng.memset(eps_t[0:1, 0:1], EPS_RMS), writes=[("eps",)])
        b.op("g", lambda eng: eng.memset(eps_t[0:1, 1:2], EPS_LN), writes=[("eps",)])
        idb = sb(es, "idb", (128, 128), BF16)
        onesb = sb(es, "onesb", (128, 1), BF16)
        onesk1b = sb(es, "onesk1b", (1, 128), BF16)
        b.dma("q", idb[:], id128b_c, writes=[("idb",)])
        b.dma("q", onesb[:], ones128_c, writes=[("onesb",)])
        b.dma("q", onesk1b[:], onesk1_c, writes=[("onesk1b",)])

        def ACT(emit, reads, writes):
            return b.op("s", emit, reads, writes)

        def DVE(emit, reads, writes):
            return b.op("v", emit, reads, writes)

        with ExitStack() as eA:
            ckv = [sb(eA, f"ckv{r}", (128, S), BF16) for r in range(2)]
            kfull = [sb(eA, f"kfull{i}", (128, S), BF16) for i in range(2)]
            qan = sb(eA, "qan", (128, DQA // 128, T), BF16)
            d1b = sb(eA, "d1b", (ROPE, ROPE), BF16)
            d2b = sb(eA, "d2b", (ROPE, ROPE), BF16)
            b.dma("q", d1b[:], d1T_c, writes=[("d1b",)])
            b.dma("q", d2b[:], d2T_c, writes=[("d2b",)])

            SEG = min(1024, S)
            NSEG = S // SEG
            ST = SEG // 512
            with ExitStack() as e1:
                xseg = sb(e1, "xseg", (128, KCH, SEG), BF16)
                wkva_sb = sb(e1, "wkva_sb", (128, KCH, KVR + ROPE), BF16)
                ckvf = sb(e1, "ckvf", (128, 2, SEG), F32)
                kpeb = sb(e1, "kpeb", (ROPE, SEG), BF16)
                sqv = sb(e1, "sqv", (128, SEG), BF16)
                rstd_b = sb(e1, "rstd_b", (1, SEG), BF16)
                ln_tmp = sb(e1, "ln_tmp", (1, SEG), F32)
                ropek = sb(e1, "ropek", (ROPE, 2, SEG), F32)
                tmpr = sb(e1, "tmpr", (ROPE, SEG), F32)
                kvlnw_sb = sb(e1, "kvlnw_sb", (128, KVR // 128), F32)
                p_acc = [psb(e1, f"p1a{i}", (128, 512)) for i in range(3)]
                p_red = [psb(e1, f"p1r{i}", (1, 512)) for i in range(ST)]
                p_bc = psb(e1, "p1bc", (128, 512))
                p_rope = [psb(e1, f"p1ro{i}", (ROPE, 512)) for i in range(2)]

                b.dma("g", wkva_sb[:], wkvaT.rearrange("c p o -> p c o"),
                      writes=[("wkva",)])
                b.dma("q", kvlnw_sb[:], kvlnw, writes=[("kvlnw",)])

                for sg in range(NSEG):
                    s0 = sg * SEG
                    for c in range(KCH):
                        b.dma("g", xseg[:, c, :],
                              xT[c * 128:(c + 1) * 128, s0:s0 + SEG],
                              writes=[("xseg", c)], sem=f"xs{c % 4}")
                    b.dma("q", ropek[:, 0, :], coskT[:, s0:s0 + SEG],
                          writes=[("ropek",)], sem="ropek")
                    b.dma("q", ropek[:, 1, :], sinkT[:, s0:s0 + SEG],
                          writes=[("ropek",)], sem="ropek")
                    for tt in range(ST):
                        for o in range(3):
                            mwid = 128 if o < 2 else ROPE
                            for c in range(KCH):
                                def mm(eng, c=c, o=o, tt=tt, mwid=mwid):
                                    return eng.matmul(
                                        p_acc[o][0:mwid, :],
                                        wkva_sb[:, c, o * 128:o * 128 + mwid],
                                        xseg[:, c, tt * 512:(tt + 1) * 512],
                                        start=(c == 0), stop=(c == KCH - 1))
                                b.op("t", mm, reads=[("xseg", c), ("wkva",)],
                                     writes=[("p1a", o)])
                            if o < 2:
                                ACT(lambda eng, o=o, tt=tt: eng.copy(
                                        out=ckvf[:, o, tt * 512:(tt + 1) * 512],
                                        in_=p_acc[o][:]),
                                    reads=[("p1a", o)], writes=[("ckvf", o, tt)])
                            else:
                                ACT(lambda eng, tt=tt: eng.copy(
                                        out=kpeb[:, tt * 512:(tt + 1) * 512],
                                        in_=p_acc[2][0:ROPE, :]),
                                    reads=[("p1a", 2)], writes=[("kpeb", tt)])
                    for tt in range(ST):
                        for o in range(2):
                            ACT(lambda eng, o=o, tt=tt: eng.square(
                                    out=sqv[:, tt * 512:(tt + 1) * 512],
                                    in_=ckvf[:, o, tt * 512:(tt + 1) * 512]),
                                reads=[("ckvf", o, tt)], writes=[("sqv", tt)])
                            def mm(eng, o=o, tt=tt):
                                return eng.matmul(
                                    p_red[tt][:], onesb[:],
                                    sqv[:, tt * 512:(tt + 1) * 512],
                                    start=(o == 0), stop=(o == 1))
                            b.op("t", mm, reads=[("sqv", tt), ("onesb",)],
                                 writes=[("p1r", tt)])
                        ACT(lambda eng, tt=tt: eng.activation(
                                out=ln_tmp[:, tt * 512:(tt + 1) * 512],
                                in_=p_red[tt][:],
                                func=AF.Ln, bias=eps_t[0:1, 0:1], scale=1.0 / KVR),
                            reads=[("p1r", tt), ("eps",)], writes=[("lnt", tt)])
                        ACT(lambda eng, tt=tt: eng.activation(
                                out=rstd_b[:, tt * 512:(tt + 1) * 512],
                                in_=ln_tmp[:, tt * 512:(tt + 1) * 512],
                                func=AF.Exp, scale=-0.5),
                            reads=[("lnt", tt)], writes=[("rstd", tt)])
                        def mmb(eng, tt=tt):
                            return eng.matmul(
                                p_bc[:], onesk1b[:],
                                rstd_b[:, tt * 512:(tt + 1) * 512],
                                start=True, stop=True)
                        b.op("t", mmb, reads=[("rstd", tt), ("onesk1b",)],
                             writes=[("p1bc",)])
                        for o in range(2):
                            DVE(lambda eng, o=o, tt=tt, t0=s0 + tt * 512:
                                eng.scalar_tensor_tensor(
                                    out=ckv[o][:, t0:t0 + 512],
                                    in0=ckvf[:, o, tt * 512:(tt + 1) * 512],
                                    scalar=kvlnw_sb[:, o:o + 1],
                                    in1=p_bc[:],
                                    op0=ALU.mult, op1=ALU.mult),
                                reads=[("ckvf", o, tt), ("p1bc",), ("kvlnw",)],
                                writes=[("ckvn",)])
                        for pi, dbuf in enumerate((d1b, d2b)):
                            def mmr(eng, pi=pi, dbuf=dbuf, tt=tt):
                                return eng.matmul(
                                    p_rope[pi][:], dbuf[:],
                                    kpeb[:, tt * 512:(tt + 1) * 512],
                                    start=True, stop=True)
                            b.op("t", mmr,
                                 reads=[("kpeb", tt), ("d1b",), ("d2b",)],
                                 writes=[("p1ro", pi)])
                        DVE(lambda eng, tt=tt: eng.tensor_mul(
                                tmpr[:, tt * 512:(tt + 1) * 512], p_rope[0][:],
                                ropek[:, 0, tt * 512:(tt + 1) * 512]),
                            reads=[("p1ro", 0), ("ropek",)], writes=[("tmpr", tt)])
                        DVE(lambda eng, tt=tt: eng.tensor_mul(
                                p_rope[1][:], p_rope[1][:],
                                ropek[:, 1, tt * 512:(tt + 1) * 512]),
                            reads=[("p1ro", 1), ("ropek",)], writes=[("p1ro", 1)])
                        for kf in range(2):
                            DVE(lambda eng, tt=tt, t0=s0 + tt * 512, kf=kf:
                                eng.tensor_add(
                                    kfull[kf][0:ROPE, t0:t0 + 512],
                                    tmpr[:, tt * 512:(tt + 1) * 512],
                                    p_rope[1][:]),
                                reads=[("tmpr", tt), ("p1ro", 1)],
                                writes=[("kfull_pe", kf)])
            b.barrier()

            with ExitStack() as e2:
                xq = sb(e2, "xq", (128, KCH, T), BF16)
                qaS = [sb(e2, f"qaS{i}", (128, D), BF16) for i in range(2)]
                qaf = sb(e2, "qaf", (128, DQA // 128, T), F32)
                sqq = sb(e2, "sqq", (128, T), BF16)
                rstdq = sb(e2, "rstdq", (1, T), BF16)
                lnq_tmp = sb(e2, "lnq_tmp", (1, T), F32)
                qlnw_sb = sb(e2, "qlnw_sb", (128, DQA // 128), F32)
                p_qa = [psb(e2, f"p2a{i}", (128, 512)) for i in range(2)]
                p_qr = [psb(e2, f"p2r{i}", (1, 512)) for i in range(TT)]
                p_qbc = psb(e2, "p2bc", (128, 512))

                b.dma("q", qlnw_sb[:], qlnw, writes=[("qlnw",)])
                for c in range(KCH):
                    b.dma("g", xq[:, c, :], xqT[c * 128:(c + 1) * 128, :],
                          writes=[("xq", c)], sem=f"xq{c % 4}")
                NO8 = DQA // 128
                for o in range(NO8):
                    b.dma("g", qaS[o % 2][:], wqaT[o * 128:(o + 1) * 128, :],
                          reads=[("qaS_c", o % 2)], writes=[("qaS", o % 2)],
                          sem=f"qaS{o % 2}")
                    for tt in range(TT):
                        for c in range(KCH):
                            def mm(eng, o=o, tt=tt, c=c):
                                return eng.matmul(
                                    p_qa[o % 2][:],
                                    qaS[o % 2][:, c * 128:(c + 1) * 128],
                                    xq[:, c, tt * 512:(tt + 1) * 512],
                                    start=(c == 0), stop=(c == KCH - 1))
                            b.op("t", mm, reads=[("xq", c), ("qaS", o % 2)],
                                 writes=[("p2a", o % 2)])
                        ACT(lambda eng, o=o, tt=tt: eng.copy(
                                out=qaf[:, o, tt * 512:(tt + 1) * 512],
                                in_=p_qa[o % 2][:]),
                            reads=[("p2a", o % 2)], writes=[("qaf", o, tt), ("qaS_c", o % 2)])
                        ACT(lambda eng, o=o, tt=tt: eng.square(
                                out=sqq[:, tt * 512:(tt + 1) * 512],
                                in_=qaf[:, o, tt * 512:(tt + 1) * 512]),
                            reads=[("qaf", o, tt)], writes=[("sqq", tt)])
                        def mm2(eng, o=o, tt=tt):
                            return eng.matmul(
                                p_qr[tt][:], onesb[:],
                                sqq[:, tt * 512:(tt + 1) * 512],
                                start=(o == 0), stop=(o == NO8 - 1))
                        b.op("t", mm2, reads=[("sqq", tt), ("onesb",)],
                             writes=[("p2r", tt)])
                for tt in range(TT):
                    ACT(lambda eng, tt=tt: eng.activation(
                            out=lnq_tmp[:, tt * 512:(tt + 1) * 512], in_=p_qr[tt][:],
                            func=AF.Ln, bias=eps_t[0:1, 0:1], scale=1.0 / DQA),
                        reads=[("p2r", tt), ("eps",)], writes=[("lnq", tt)])
                    ACT(lambda eng, tt=tt: eng.activation(
                            out=rstdq[:, tt * 512:(tt + 1) * 512],
                            in_=lnq_tmp[:, tt * 512:(tt + 1) * 512],
                            func=AF.Exp, scale=-0.5),
                        reads=[("lnq", tt)], writes=[("rstdq", tt)])
                    def mmb(eng, tt=tt):
                        return eng.matmul(p_qbc[:], onesk1b[:],
                                          rstdq[:, tt * 512:(tt + 1) * 512],
                                          start=True, stop=True)
                    b.op("t", mmb, reads=[("rstdq", tt), ("onesk1b",)],
                         writes=[("p2bc",)])
                    for o in range(NO8):
                        DVE(lambda eng, o=o, tt=tt: eng.scalar_tensor_tensor(
                                out=qan[:, o, tt * 512:(tt + 1) * 512],
                                in0=qaf[:, o, tt * 512:(tt + 1) * 512],
                                scalar=qlnw_sb[:, o:o + 1],
                                in1=p_qbc[:],
                                op0=ALU.mult, op1=ALU.mult),
                            reads=[("qaf", o, tt), ("p2bc",), ("qlnw",)],
                            writes=[("qan",)])
            b.barrier()

            with ExitStack() as e3:
                vbuf = sb(e3, "vbuf", (128, NKT * 4, 512), BF16)
                attnb = sb(e3, "attn0", (128, NKT * 512), BF16)
                attnT = sb(e3, "attnT", (128, 16, 128), BF16)
                qh = sb(e3, "qh0", (128, T), BF16)
                qpe = sb(e3, "qpe", (ROPE, T), BF16)
                wqb_h = [sb(e3, "wqbh0", (128, DQA // 128, 128), BF16)]
                wkn = [sb(e3, f"wkn{i}", (128, KVR // 128, NOPE), BF16)
                       for i in range(2)]
                wkv = [sb(e3, f"wkv{i}", (128, KVR // 128, 512), BF16)
                       for i in range(2)]
                ropeq = sb(e3, "ropeq", (ROPE, 2, T), BF16)
                tmpq = sb(e3, "tmpq", (ROPE, 512), BF16)
                maskA_sb = sb(e3, "maskA_sb", (128, 512), BF16)
                maskB_sb = sb(e3, "maskB_sb", (128, 512), BF16)
                mx = sb(e3, "mx", (128, max(NKT, 2)), F32)
                rs = sb(e3, "rs", (128, 4, max(NKT + 1, 2)), F32)
                nmax = [sb(e3, f"nmax{i}", (128, 1), F32) for i in range(2)]
                biasq = [sb(e3, f"biasq{i}", (128, 1), F32) for i in range(2)]
                rsum = sb(e3, "rsum", (128, 1), F32)
                rcpc = sb(e3, "rcpc", (128, 1), BF16)
                rrowb = sb(e3, "rrowb", (1, 128), BF16)
                rnorm = sb(e3, "rnorm", (128, 512), BF16)
                avsb = sb(e3, "avsb", (128, 512), BF16)
                p_sc4 = psb(e3, "p3s4", (128, 2048))
                p_sc = [p_sc4.ap()[:, i * 512:(i + 1) * 512] for i in range(4)]
                p_dg = [psb(e3, "p3d0", (128, 512), BF16)]
                p_av = psb(e3, "p3av", (128, 512))
                p_dec = [psb(e3, f"p3k{i}", (128, 512)) for i in range(2)]

                b.dma("q", maskA_sb[:], maskA, writes=[("maskA",)])
                b.dma("q", maskB_sb[:], maskB, writes=[("maskB",)])
                b.dma("g", ropeq[:, 0, :], cosqT, writes=[("ropeq",)])
                b.dma("g", ropeq[:, 1, :], sinqT, writes=[("ropeq",)])

                wkvbT_r = wkvbT.rearrange("c p (hh x) -> p c hh x", x=NOPE + VD)
                NO8 = DQA // 128
                PGRP = 4 if NP >= 4 else NP
                pgroups = [list(range(g, min(g + PGRP, NP)))
                           for g in range(0, NP, PGRP)]
                scb = [0]

                class Pending:
                    def __init__(self, h, hl, hp, m, pl, last_in_grp, grp, bA):
                        par = m % 2
                        DVE(lambda eng, par=par: eng.reduce_max(
                                nmax[par][:], mx[:, 0:2], axis=AX,
                                negate=True),
                            reads=[("mx",)], writes=[("nmax", par)])
                        DVE(lambda eng, par=par: eng.tensor_scalar_mul(
                                biasq[par][:], nmax[par][:], SCALE),
                            reads=[("nmax", par)], writes=[("biasq", par)])

                        def rowsum_ops(m=m, pl=pl):
                            DVE(lambda eng, m=m, pl=pl: eng.reduce_sum(
                                    rsum[:], rs[:, pl, 0:m + 1], axis=AX),
                                reads=[("rs", pl)], writes=[("rsum",)])
                            DVE(lambda eng: eng.reciprocal(rsum[:], rsum[:]),
                                reads=[("rsum",)], writes=[("rsum",)])
                            DVE(lambda eng: eng.tensor_copy(out=rcpc[:], in_=rsum[:]),
                                reads=[("rsum",)], writes=[("rcpc",)])
                            def ttr(eng):
                                return eng.transpose(p_dg[0][0:1, 0:128], rcpc[:],
                                                     idb[:])
                            b.op("t", ttr, reads=[("rcpc",), ("idb",)],
                                 writes=[("p3d", 0)])
                            ACT(lambda eng: eng.copy(out=rrowb[:],
                                                     in_=p_dg[0][0:1, 0:128]),
                                reads=[("p3d", 0)], writes=[("rrowb",)])
                            def tbc(eng):
                                return eng.matmul(p_sc[2][:, 0:128], onesk1b[:],
                                                  rrowb[:], start=True, stop=True)
                            b.op("t", tbc, reads=[("rrowb",), ("onesk1b",)],
                                 writes=[("p3s", 2)])
                            ACT(lambda eng, pl=pl: eng.copy(
                                    out=rnorm[:, pl * 128:(pl + 1) * 128],
                                    in_=p_sc[2][:, 0:128]),
                                reads=[("p3s", 2)], writes=[("rnorm",)])
                        grp_rowsums.append(rowsum_ops)

                        nkj = 2 * m * 4
                        q0 = (2 * m - 2) * 64
                        chunk_t0 = [2 * m - 2] + [2 * c - 2 for c in range(1, m)]
                        kj_pos = []
                        for t0 in chunk_t0:
                            kj_pos += [t0 * 4 + j for j in range(8)]
                        t0d = chunk_t0[0]
                        for d in range(2):
                            ACT(lambda eng, t0d=t0d, d=d, par=par,
                                pl=pl, m=m: eng.activation(
                                    out=attnb[:, (t0d + d) * 512:
                                              (t0d + d + 1) * 512],
                                    in_=p_dec[d][:],
                                    func=AF.Exp, bias=biasq[par][:],
                                    scale=SCALE,
                                    accum_out=rs[:, pl, d * m:d * m + 1]),
                                reads=[("biasq", par), ("p3k", d)],
                                writes=[("attn", t0d + d), ("rs", pl)])
                        self.steps = []
                        for s in range(1, m + 3):
                            def step(s=s, m=m, par=par, pl=pl, hl=hl, hp=hp,
                                     nkj=nkj, h=h, q0=q0, grp=grp, bA=bA,
                                     chunk_t0=chunk_t0, kj_pos=kj_pos,
                                     last=(last_in_grp and s == m + 2)):
                                if s < m:
                                    t0 = chunk_t0[s]
                                    if True:
                                        scb[0] += 2
                                        bank = scb[0] % 4
                                        for d in range(2):
                                            def mmb(eng, t0=t0, d=d, q0=q0,
                                                    hp=hp, bank=bank):
                                                return eng.matmul(
                                                    p_sc[bank + d][:],
                                                    qh[:, q0:q0 + 128],
                                                    kfull[hp][:, (t0 + d) * 512:
                                                               (t0 + d + 1) * 512],
                                                    start=True, stop=True)
                                            b.op("t", mmb,
                                                 reads=[("qh",), ("kfull", hp),
                                                        ("kfull_pe", hp)],
                                                 writes=[("p3s", bank + d)])
                                        ACT(lambda eng, t0=t0, s=s, par=par, pl=pl,
                                            bank=bank: eng.activation(
                                                out=attnb[:, t0 * 512:(t0 + 2) * 512],
                                                in_=p_sc4.ap()[:, bank * 512:
                                                               (bank + 2) * 512],
                                                func=AF.Exp, bias=biasq[par][:],
                                                scale=SCALE,
                                                accum_out=rs[:, pl, s:s + 1]),
                                            reads=[("biasq", par), ("p3s", bank),
                                                   ("p3s", bank + 1)],
                                            writes=[("attn", t0), ("attn", t0 + 1),
                                                    ("rs", pl)])
                                def trans_half(d, s=s):
                                    t0_ = chunk_t0[s - 2]
                                    td = t0_ + d
                                    base_pos = ((s - 2) * 8 + d * 4) % 16
                                    for j in range(4):
                                        def mmd(eng, td=td, j=j):
                                            kk = td * 4 + j
                                            return eng.transpose(
                                                p_dg[0][:, j * 128:(j + 1) * 128],
                                                attnb[:, kk * 128:(kk + 1) * 128],
                                                idb[:])
                                        b.op("t", mmd,
                                             reads=[("attn", td), ("idb",)],
                                             writes=[("p3d", 0)])
                                    slots = [base_pos + j for j in range(4)]
                                    if d == 0:
                                        DVE(lambda eng, s0=slots[0]: eng.tensor_copy(
                                                out=attnT[:, s0:s0 + 4, :],
                                                in_=p_dg[0][:].rearrange(
                                                    "p (j q) -> p j q", j=4)),
                                            reads=[("p3d", 0)],
                                            writes=[("attnT", s_) for s_ in slots])
                                    else:
                                        DVE(lambda eng, s0=slots[0]: eng.tensor_copy(
                                                out=attnT[:, s0:s0 + 4, :],
                                                in_=p_dg[0][:].rearrange(
                                                    "p (j q) -> p j q", j=4)),
                                            reads=[("p3d", 0)],
                                            writes=[("attnT", s_) for s_ in slots])
                                if 2 <= s <= m + 1:
                                    trans_half(0)
                                if s >= 3:
                                    for x in range(8):
                                        pos = (s - 3) * 8 + x
                                        kj = kj_pos[pos]
                                        slot = pos % 16
                                        def mma(eng, kj=kj, slot=slot, pl=pl,
                                                hl=hl, pos=pos, nkj=nkj):
                                            return eng.matmul(
                                                p_av[:, pl * 128:(pl + 1) * 128],
                                                vbuf[:, kj, hl * 128:(hl + 1) * 128],
                                                attnT[:, slot, :],
                                                start=(pos == 0),
                                                stop=(pos == nkj - 1))
                                        b.op("t", mma,
                                             reads=[("vbuf",), ("attnT", slot)],
                                             writes=[("p3av",)])
                                if 2 <= s <= m + 1:
                                    trans_half(1)
                                if last:
                                    for rop in grp_rowsums:
                                        rop()
                                    del grp_rowsums[:]
                                    w = len(grp) * 128
                                    DVE(lambda eng, w=w: eng.tensor_mul(
                                            avsb[:, 0:w], p_av[:, 0:w],
                                            rnorm[:, 0:w]),
                                        reads=[("p3av",), ("rnorm",)],
                                        writes=[("avsb",)])
                                    c0 = grp[0] * 128
                                    b.dma("q",
                                          attT_sc[h * 128:(h + 1) * 128,
                                                  c0:c0 + w],
                                          avsb[:, 0:w], reads=[("avsb",)],
                                          writes=[("attT_sc",)], sem="sc_at")
                            self.steps.append(step)
                        self.steps.reverse()

                    def step(self):
                        if self.steps:
                            self.steps.pop()()

                    def drain(self):
                        while self.steps:
                            self.steps.pop()()

                pend = [None]
                grp_rowsums = []

                def pstep():
                    if pend[0] is not None:
                        pend[0].step()
                for hg in range(H // 4):
                    h0 = 4 * hg
                    for hh in range(4):
                        b.dma("g", wkv[hg % 2][:, :, hh * VD:(hh + 1) * VD],
                              wkvbT_r[:, :, h0 + hh, NOPE:NOPE + VD],
                              reads=[("wkv_c", hg % 2)], writes=[("wkv", hg % 2)],
                              sem=f"wkv{hg % 2}")
                    for kj in range(NKT * 4):
                        for r in range(2):
                            def mmv(eng, kj=kj, r=r, hg=hg):
                                return eng.matmul(
                                    p_dec[kj % 2][:],
                                    ckv[r][:, kj * 128:(kj + 1) * 128],
                                    wkv[hg % 2][:, r, :],
                                    start=(r == 0), stop=(r == 1))
                            b.op("t", mmv, reads=[("ckvn",), ("wkv", hg % 2)],
                                 writes=[("p3k", kj % 2)])
                        ev_ = ACT if kj % 2 else DVE
                        ev_(lambda eng, kj=kj: (
                                eng.copy(out=vbuf[:, kj, :], in_=p_dec[kj % 2][:])
                                if hasattr(eng, "copy") and not hasattr(eng, "tensor_copy")
                                else eng.tensor_copy(out=vbuf[:, kj, :],
                                                     in_=p_dec[kj % 2][:])),
                            reads=[("p3k", kj % 2)],
                            writes=[("vbuf",)] +
                                   ([("wkv_c", hg % 2)] if kj == NKT * 4 - 1 else []))
                        pstep()
                    for hl in range(4):
                        h = h0 + hl
                        hp = h % 2
                        hq = h % 2
                        b.dma("g", wkn[hq][:],
                              wkvbT_r[:, :, h, 0:NOPE],
                              reads=[("wkn_c", hq)], writes=[("wkn", hq)],
                              sem=f"wkn{hq}")
                        b.dma("g", wqb_h[0][:],
                              wqbT.rearrange("c p o -> p c o")[:, :, h * 128:(h + 1) * 128],
                              reads=[("wqbh_c", 0)], writes=[("wqbh", 0)],
                              sem="wqb0")
                        for tt in range(NKT):
                            for r in range(2):
                                def mmk(eng, tt=tt, r=r, hq=hq):
                                    return eng.matmul(
                                        p_dec[tt % 2][64:128, :],
                                        wkn[hq][:, r, :],
                                        ckv[r][:, tt * 512:(tt + 1) * 512],
                                        start=(r == 0), stop=(r == 1),
                                        tile_position=(0, 64))
                                b.op("t", mmk, reads=[("ckvn",), ("wkn", hq)],
                                     writes=[("p3k", tt % 2)])
                            ACT(lambda eng, tt=tt, hp=hp: eng.copy(
                                    out=kfull[hp][64:128, tt * 512:(tt + 1) * 512],
                                    in_=p_dec[tt % 2][64:128, :]),
                                reads=[("p3k", tt % 2)],
                                writes=[("kfull", hp)] +
                                       ([("wkn_c", hq)] if tt == NKT - 1 else []))
                            pstep()
                        for tt in range(TT):
                            for c in range(NO8):
                                def mmq(eng, tt=tt, c=c):
                                    return eng.matmul(
                                        p_dec[tt % 2][:],
                                        wqb_h[0][:, c, :],
                                        qan[:, c, tt * 512:(tt + 1) * 512],
                                        start=(c == 0), stop=(c == NO8 - 1))
                                b.op("t", mmq, reads=[("qan",), ("wqbh", 0)],
                                     writes=[("p3k", tt % 2)])
                            ACT(lambda eng, tt=tt: eng.copy(
                                    out=qpe[:, tt * 512:(tt + 1) * 512],
                                    in_=p_dec[tt % 2][0:ROPE, :]),
                                reads=[("p3k", tt % 2)],
                                writes=[("qpe", tt)])
                            ACT(lambda eng, tt=tt: eng.copy(
                                    out=qh[ROPE:128, tt * 512:(tt + 1) * 512],
                                    in_=p_dec[tt % 2][ROPE:128, :]),
                                reads=[("p3k", tt % 2)],
                                writes=[("qh",), ("wqbh_c", 0)])
                            pstep()
                        for tt in range(TT):
                            for pi, dbuf in enumerate((d1b, d2b)):
                                def mmr(eng, pi=pi, dbuf=dbuf, tt=tt):
                                    return eng.matmul(
                                        p_dec[pi][0:ROPE, :], dbuf[:],
                                        qpe[:, tt * 512:(tt + 1) * 512],
                                        start=True, stop=True)
                                b.op("t", mmr,
                                     reads=[("qpe", tt), ("d1b",), ("d2b",)],
                                     writes=[("p3k", pi)])
                            DVE(lambda eng, tt=tt: eng.tensor_mul(
                                    tmpq[:, 0:512],
                                    p_dec[1][0:ROPE, :],
                                    ropeq[:, 1, tt * 512:(tt + 1) * 512]),
                                reads=[("p3k", 1), ("ropeq",)], writes=[("tmpq", tt)])
                            DVE(lambda eng, tt=tt: eng.tensor_mul(
                                    p_dec[0][0:ROPE, :],
                                    p_dec[0][0:ROPE, :],
                                    ropeq[:, 0, tt * 512:(tt + 1) * 512]),
                                reads=[("p3k", 0), ("ropeq",)], writes=[("p3k", 0)])
                            DVE(lambda eng, tt=tt: eng.tensor_add(
                                    qh[0:ROPE, tt * 512:(tt + 1) * 512],
                                    p_dec[0][0:ROPE, :],
                                    tmpq[:, 0:512]),
                                reads=[("tmpq", tt), ("p3k", 0)], writes=[("qh",)])
                            pstep()
                        for grp in pgroups:
                            for pl, pz in enumerate(grp):
                                m = pz + 1
                                q0 = (2 * m - 2) * 64
                                for sl_i, kt in enumerate((2 * m - 2, 2 * m - 1)):
                                    def mms(eng, kt=kt, q0=q0, hp=hp, sl_i=sl_i):
                                        return eng.matmul(
                                            p_dec[sl_i][:],
                                            qh[:, q0:q0 + 128],
                                            kfull[hp][:, kt * 512:(kt + 1) * 512],
                                            start=True, stop=True)
                                    b.op("t", mms,
                                         reads=[("qh",), ("kfull", hp),
                                                ("kfull_pe", hp)],
                                         writes=[("p3k", sl_i)])
                                    msk = maskA_sb if sl_i == 0 else maskB_sb
                                    mk = ("maskA",) if sl_i == 0 else ("maskB",)
                                    DVE(lambda eng, sl_i=sl_i, msk=msk:
                                        eng.scalar_tensor_tensor(
                                            out=p_dec[sl_i][:],
                                            in0=p_dec[sl_i][:], scalar=1.0,
                                            in1=msk[:],
                                            op0=ALU.mult, op1=ALU.add),
                                        reads=[("p3k", sl_i), mk],
                                        writes=[("p3k", sl_i)])
                                    DVE(lambda eng, sl_i=sl_i: eng.reduce_max(
                                            mx[:, sl_i:sl_i + 1],
                                            p_dec[sl_i][:],
                                            axis=AX),
                                        reads=[("p3k", sl_i)],
                                        writes=[("mx",)])
                                    pstep()
                                if pend[0] is not None:
                                    pend[0].drain()
                                pend[0] = Pending(h, hl, hp, m, pl,
                                                  pz == grp[-1], grp, 0)
                if pend[0] is not None:
                    pend[0].drain()
            b.barrier()

        with ExitStack() as eB:
            x1buf = sb(eB, "x1buf", (128, KCH, T), BF16)
            esl = [sb(eB, f"esl{i}", (NE, 128), BF16) for i in range(2)]
            gw_sb = sb(eB, "gw_sb", (128, KCH, NE), BF16)
            id16b = sb(eB, "id16b", (16, 16), F32)
            id128fb = sb(eB, "id128fb", (128, 128), F32)
            b.dma("g", gw_sb[:], gwT.rearrange("c p o -> p c o"), writes=[("gw",)])
            b.dma("q", id16b[:], id16f_c, writes=[("id16",)])
            b.dma("q", id128fb[:], id128f_c, writes=[("id128f",)])

            with ExitStack() as e4:
                stat_m = sb(e4, "stat_m", (1, T), F32)
                stat_v = sb(e4, "stat_v", (1, T), F32)
                stat_l = sb(e4, "stat_l", (1, T), F32)
                rstd4 = sb(e4, "rstd4", (1, T), BF16)
                nmrs4 = sb(e4, "nmrs4", (1, T), BF16)
                Asb = sb(e4, "Asb", (128, 2, 512), F32)
                Bsb = sb(e4, "Bsb", (128, 2, 512), F32)
                attT = sb(e4, "attT", (128, KCH, T), BF16)
                woStrip = [sb(e4, f"woS{i}", (128, D), BF16) for i in range(2)]
                xres = [sb(e4, f"xres{i}", (128, T), F32) for i in range(2)]
                sq4 = sb(e4, "sq4", (128, T), BF16)
                p_wo = [psb(e4, f"p4w{i}", (128, 512)) for i in range(2)]
                p_sx = [psb(e4, f"p4sx{i}", (1, 512)) for i in range(TT)]
                p_sq = [psb(e4, f"p4sq{i}", (1, 512)) for i in range(TT)]
                p_bc4 = [psb(e4, f"p4bc{i}", (128, 512)) for i in range(2)]

                for c in range(KCH):
                    b.dma("q", attT[:, c, :], attT_sc[c * 128:(c + 1) * 128, :],
                          reads=[("attT_sc",)], writes=[("attT", c)],
                          sem=f"at{c % 4}")
                for o in range(KCH):
                    os2 = o % 2
                    b.dma("g", woStrip[os2][:], woTs[o * 128:(o + 1) * 128, :],
                          reads=[("woS_c", os2)], writes=[("woS", os2)],
                          sem=f"woS{os2}")
                    b.dma("q", xres[os2][:], xqT[o * 128:(o + 1) * 128, :],
                          reads=[("xres_c", os2)], writes=[("xres", os2)],
                          sem=f"xr{os2}")
                    for tt in range(TT):
                        for c in range(KCH):
                            def mmw(eng, os2=os2, tt=tt, c=c):
                                return eng.matmul(
                                    p_wo[os2][:],
                                    woStrip[os2][:, c * 128:(c + 1) * 128],
                                    attT[:, c, tt * 512:(tt + 1) * 512],
                                    start=(c == 0), stop=(c == KCH - 1))
                            b.op("t", mmw, reads=[("woS", os2), ("attT", c)],
                                 writes=[("p4w", os2)])
                        DVE(lambda eng, os2=os2, tt=tt, o=o: eng.tensor_add(
                                x1buf[:, o, tt * 512:(tt + 1) * 512],
                                p_wo[os2][:],
                                xres[os2][:, tt * 512:(tt + 1) * 512]),
                            reads=[("p4w", os2), ("xres", os2)],
                            writes=[("x1pre", o, tt), ("xres_c", os2)])
                        ACT(lambda eng, o=o, tt=tt: eng.square(
                                out=sq4[:, tt * 512:(tt + 1) * 512],
                                in_=x1buf[:, o, tt * 512:(tt + 1) * 512]),
                            reads=[("x1pre", o, tt)], writes=[("sq4", tt)])
                        def mmsx(eng, o=o, tt=tt):
                            return eng.matmul(
                                p_sx[tt][:], onesb[:],
                                x1buf[:, o, tt * 512:(tt + 1) * 512],
                                start=(o == 0), stop=(o == KCH - 1))
                        b.op("t", mmsx, reads=[("x1pre", o, tt), ("onesb",)],
                             writes=[("p4sx", tt)])
                        def mmsq(eng, o=o, tt=tt):
                            return eng.matmul(p_sq[tt][:], onesb[:],
                                              sq4[:, tt * 512:(tt + 1) * 512],
                                              start=(o == 0), stop=(o == KCH - 1))
                        b.op("t", mmsq, reads=[("sq4", tt), ("onesb",)],
                             writes=[("p4sq", tt)])
                for tt in range(TT):
                    sl = slice(tt * 512, (tt + 1) * 512)
                    ACT(lambda eng, sl=sl, tt=tt: eng.activation(
                            out=stat_m[:, sl], in_=p_sx[tt][:], func=AF.Copy,
                            scale=1.0 / D),
                        reads=[("p4sx", tt)], writes=[("st4m", tt)])
                    ACT(lambda eng, sl=sl, tt=tt: eng.activation(
                            out=stat_v[:, sl], in_=p_sq[tt][:], func=AF.Copy,
                            scale=1.0 / D),
                        reads=[("p4sq", tt)], writes=[("st4v", tt)])
                    ACT(lambda eng, sl=sl, tt=tt: eng.square(
                            out=stat_l[:, sl], in_=stat_m[:, sl]),
                        reads=[("st4m", tt)], writes=[("st4l", tt)])
                    DVE(lambda eng, sl=sl: eng.tensor_sub(
                            stat_v[:, sl], stat_v[:, sl], stat_l[:, sl]),
                        reads=[("st4v", tt), ("st4l", tt)], writes=[("st4v", tt)])
                    ACT(lambda eng, sl=sl, tt=tt: eng.activation(
                            out=stat_l[:, sl], in_=stat_v[:, sl], func=AF.Ln,
                            bias=eps_t[0:1, 1:2]),
                        reads=[("st4v", tt), ("eps",)], writes=[("st4l", tt)])
                    ACT(lambda eng, sl=sl, tt=tt: eng.activation(
                            out=rstd4[:, sl], in_=stat_l[:, sl], func=AF.Exp,
                            scale=-0.5),
                        reads=[("st4l", tt)], writes=[("rstd4", tt)])
                    DVE(lambda eng, sl=sl: eng.tensor_mul(
                            stat_m[:, sl], stat_m[:, sl], rstd4[:, sl]),
                        reads=[("st4m", tt), ("rstd4", tt)], writes=[("st4m", tt)])
                    DVE(lambda eng, sl=sl: eng.tensor_scalar_mul(
                            nmrs4[:, sl], stat_m[:, sl], -1.0),
                        reads=[("st4m", tt)], writes=[("nmrs4", tt)])
                    def mba(eng, sl=sl):
                        return eng.matmul(p_bc4[0][:], onesk1b[:], rstd4[:, sl],
                                          start=True, stop=True)
                    b.op("t", mba, reads=[("rstd4", tt), ("onesk1b",)],
                         writes=[("p4bcA",)])
                    def mbb(eng, sl=sl):
                        return eng.matmul(p_bc4[1][:], onesk1b[:], nmrs4[:, sl],
                                          start=True, stop=True)
                    b.op("t", mbb, reads=[("nmrs4", tt), ("onesk1b",)],
                         writes=[("p4bcB",)])
                    ACT(lambda eng, tt=tt: eng.copy(out=Asb[:, tt, :],
                                                    in_=p_bc4[0][:]),
                        reads=[("p4bcA",)], writes=[("Asb", tt)])
                    ACT(lambda eng, tt=tt: eng.copy(out=Bsb[:, tt, :],
                                                    in_=p_bc4[1][:]),
                        reads=[("p4bcB",)], writes=[("Bsb", tt)])
                    for c in range(KCH):
                        DVE(lambda eng, c=c, tt=tt: eng.tensor_mul(
                                x1buf[:, c, tt * 512:(tt + 1) * 512],
                                x1buf[:, c, tt * 512:(tt + 1) * 512],
                                Asb[:, tt, :]),
                            reads=[("x1pre", c, tt), ("Asb", tt)],
                            writes=[("x1pre", c, tt)])
                        DVE(lambda eng, c=c, tt=tt: eng.tensor_add(
                                x1buf[:, c, tt * 512:(tt + 1) * 512],
                                x1buf[:, c, tt * 512:(tt + 1) * 512],
                                Bsb[:, tt, :]),
                            reads=[("x1pre", c, tt), ("Bsb", tt)],
                            writes=[("x1", c, tt)])
            b.barrier()

            for th in range(TT):
                tsl = slice(th * 512, (th + 1) * 512)
                with ExitStack() as eH:
                    hw = sb(eH, "hw", (128, NIC, 512), BF16)
                    with ExitStack() as e5:
                        lgT = sb(e5, "lgT", (16, 512), F32)
                        lgtm = sb(e5, "lgtm", (128, 4, 16), F32)
                        lgr = sb(e5, "lgr", (128, 4, 16), F32)
                        m1 = sb(e5, "m1", (128, 4), F32)
                        m2 = sb(e5, "m2", (128, 4), F32)
                        w1 = sb(e5, "w1", (128, 4), F32)
                        w2 = sb(e5, "w2", (128, 4), F32)
                        msk1 = sb(e5, "msk1", (128, 4, 16), F32)
                        msk2 = sb(e5, "msk2", (128, 4, 16), F32)
                        wgt = sb(e5, "wgt", (128, 4, 16), F32)
                        wTf = sb(e5, "wTf", (16, 512), F32)
                        wTb = sb(e5, "wTb", (16, 512), BF16)
                        uS = [sb(e5, f"uS{i}", (128, D), BF16) for i in range(2)]
                        gS = [sb(e5, f"gS{i}", (128, D), BF16) for i in range(2)]
                        Wm = sb(e5, "Wm", (128, 512), BF16)
                        sg_sb = sb(e5, "sg_sb", (128, 512), BF16)
                        ht_sb = sb(e5, "ht_sb", (128, 512), BF16)
                        p_u = [psb(e5, f"p5u{i}", (128, 512)) for i in range(2)]
                        p_g = [psb(e5, f"p5g{i}", (128, 512)) for i in range(2)]
                        p_wm = psb(e5, "p5wm", (128, 512))
                        p_gt = psb(e5, "p5gt", (128, 512))

                        for c in range(KCH):
                            def mmg(eng, c=c, tsl=tsl):
                                return eng.matmul(p_gt[0:16, :], gw_sb[:, c, :],
                                                  x1buf[:, c, tsl],
                                                  start=(c == 0),
                                                  stop=(c == KCH - 1))
                            b.op("t", mmg, reads=[("x1", c, th), ("gw",)],
                                 writes=[("p5gt",)])
                        DVE(lambda eng: eng.tensor_copy(out=lgT[:],
                                                        in_=p_gt[0:16, :]),
                            reads=[("p5gt",)], writes=[("lgT",)])
                        for sub in range(4):
                            def mtr(eng, sub=sub):
                                return eng.transpose(
                                    p_gt[:, 0:16],
                                    lgT[:, sub * 128:(sub + 1) * 128],
                                    id16b[:])
                            b.op("t", mtr, reads=[("lgT",), ("id16",)],
                                 writes=[("p5gt",)])
                            DVE(lambda eng, sub=sub: eng.tensor_copy(
                                    out=lgtm[:, sub, :], in_=p_gt[:, 0:16]),
                                reads=[("p5gt",)], writes=[("lgtm", sub)])
                        rall = [("lgtm", s) for s in range(4)]
                        DVE(lambda eng: eng.reduce_max(m1[:], lgtm.ap(), axis=AX),
                            reads=rall, writes=[("m1",)])
                        for sub in range(4):
                            DVE(lambda eng, sub=sub: eng.tensor_single_scalar(
                                    out=msk1[:, sub, :], in_=lgtm[:, sub, :],
                                    scalar=m1[:, sub:sub + 1], op=ALU.is_ge),
                                reads=rall + [("m1",)], writes=[("msk1",)])
                        DVE(lambda eng: eng.scalar_tensor_tensor(
                                out=lgr.ap(), in0=msk1.ap(), scalar=NEG,
                                in1=lgtm.ap(), op0=ALU.mult, op1=ALU.add),
                            reads=[("msk1",)] + rall, writes=[("lgr",)])
                        DVE(lambda eng: eng.reduce_max(m2[:], lgr.ap(), axis=AX),
                            reads=[("lgr",)], writes=[("m2",)])
                        for sub in range(4):
                            ACT(lambda eng, sub=sub: eng.activation(
                                    out=w1[:, sub:sub + 1], in_=m2[:, sub:sub + 1],
                                    func=AF.Sigmoid, bias=m1[:, sub:sub + 1],
                                    scale=-1.0),
                                reads=[("m1",), ("m2",)], writes=[("w1",)])
                        DVE(lambda eng: eng.tensor_scalar(
                                out=w2[:], in0=w1[:], scalar1=-1.0, scalar2=1.0,
                                op0=ALU.mult, op1=ALU.add),
                            reads=[("w1",)], writes=[("w2",)])
                        for sub in range(4):
                            DVE(lambda eng, sub=sub: eng.tensor_single_scalar(
                                    out=msk2[:, sub, :], in_=lgr[:, sub, :],
                                    scalar=m2[:, sub:sub + 1], op=ALU.is_ge),
                                reads=[("lgr",), ("m2",)], writes=[("msk2",)])
                            DVE(lambda eng, sub=sub: eng.tensor_scalar_mul(
                                    msk1[:, sub, :], msk1[:, sub, :],
                                    w1[:, sub:sub + 1]),
                                reads=[("msk1",), ("w1",)], writes=[("msk1",)])
                            DVE(lambda eng, sub=sub: eng.tensor_scalar_mul(
                                    msk2[:, sub, :], msk2[:, sub, :],
                                    w2[:, sub:sub + 1]),
                                reads=[("msk2",), ("w2",)], writes=[("msk2",)])
                        DVE(lambda eng: eng.tensor_add(wgt.ap(), msk1.ap(),
                                                       msk2.ap()),
                            reads=[("msk1",), ("msk2",)], writes=[("wgt",)])
                        for sub in range(4):
                            def mtr2(eng, sub=sub):
                                return eng.transpose(p_gt[0:16, 0:128],
                                                     wgt[:, sub, :],
                                                     id128fb[:])
                            b.op("t", mtr2, reads=[("wgt",), ("id128f",)],
                                 writes=[("p5gt",)])
                            DVE(lambda eng, sub=sub: eng.tensor_copy(
                                    out=wTf[:, sub * 128:(sub + 1) * 128],
                                    in_=p_gt[0:16, 0:128]),
                                reads=[("p5gt",)], writes=[("wTf",)])
                        DVE(lambda eng: eng.tensor_copy(out=wTb[:], in_=wTf[:]),
                            reads=[("wTf",)], writes=[("wTb",)])
                        for ot in range(NIC):
                            ot2 = ot % 2
                            b.dma("g", uS[ot2][:], uTs[ot * 128:(ot + 1) * 128, :],
                                  reads=[("uS_c", ot2)], writes=[("uS", ot2)],
                                  sem=f"uS{ot2}")
                            b.dma("g", gS[ot2][:], gTs[ot * 128:(ot + 1) * 128, :],
                                  reads=[("gS_c", ot2)], writes=[("gS", ot2)],
                                  sem=f"gS{ot2}")
                            for c in range(KCH):
                                def mmu(eng, ot2=ot2, c=c, tsl=tsl):
                                    return eng.matmul(
                                        p_u[ot2][:],
                                        uS[ot2][:, c * 128:(c + 1) * 128],
                                        x1buf[:, c, tsl],
                                        start=(c == 0), stop=(c == KCH - 1))
                                b.op("t", mmu, reads=[("uS", ot2), ("x1", c, th)],
                                     writes=[("p5u", ot2)])
                                def mmg2(eng, ot2=ot2, c=c, tsl=tsl):
                                    return eng.matmul(
                                        p_g[ot2][:],
                                        gS[ot2][:, c * 128:(c + 1) * 128],
                                        x1buf[:, c, tsl],
                                        start=(c == 0), stop=(c == KCH - 1))
                                b.op("t", mmg2, reads=[("gS", ot2), ("x1", c, th)],
                                     writes=[("p5g", ot2)])
                            b.dma("q", esl[ot2][:],
                                  esel_c[:, ot * 128:(ot + 1) * 128],
                                  reads=[("esl_c", ot2)], writes=[("esl", ot2)],
                                  sem=f"esl{ot2}")
                            def mmwm(eng, ot2=ot2):
                                return eng.matmul(p_wm[:], esl[ot2][:],
                                                  wTb[:], start=True, stop=True)
                            b.op("t", mmwm, reads=[("esl", ot2), ("wTb",)],
                                 writes=[("p5wm",), ("esl_c", ot2)])
                            ACT(lambda eng: eng.copy(out=Wm[:], in_=p_wm[:]),
                                reads=[("p5wm",)], writes=[("Wm",)])
                            ACT(lambda eng, ot2=ot2: eng.activation(
                                    out=sg_sb[:], in_=p_g[ot2][:], func=AF.Silu),
                                reads=[("p5g", ot2)], writes=[("sg",), ("gS_c", ot2)])
                            DVE(lambda eng, ot2=ot2: eng.tensor_mul(
                                    ht_sb[:], p_u[ot2][:], sg_sb[:]),
                                reads=[("p5u", ot2), ("sg",)],
                                writes=[("ht",), ("uS_c", ot2)])
                            DVE(lambda eng, ot=ot: eng.tensor_mul(
                                    hw[:, ot, :], ht_sb[:], Wm[:]),
                                reads=[("ht",), ("Wm",)], writes=[("hw", ot)])
                    b.barrier()
                    with ExitStack() as e6:
                        dS = [sb(e6, f"dS{i}", (128, DI), BF16) for i in range(2)]
                        x2pre = sb(e6, "x2pre", (128, KCH, 512), BF16)
                        sq6 = sb(e6, "sq6", (128, 512), BF16)
                        st6_m = sb(e6, "st6_m", (1, 512), F32)
                        st6_v = sb(e6, "st6_v", (1, 512), F32)
                        st6_l = sb(e6, "st6_l", (1, 512), F32)
                        rstd6 = sb(e6, "rstd6", (1, 512), BF16)
                        nmrs6 = sb(e6, "nmrs6", (1, 512), BF16)
                        A6 = sb(e6, "A6", (128, 512), BF16)
                        B6 = sb(e6, "B6", (128, 512), BF16)
                        outsb = sb(e6, "outsb", (128, 512), F32)
                        p_d = [psb(e6, f"p6d{i}", (128, 512)) for i in range(2)]
                        p_s6 = [psb(e6, f"p6s{i}", (1, 512)) for i in range(2)]
                        p_b6 = psb(e6, "p6b", (128, 512))

                        for o in range(KCH):
                            os2 = o % 2
                            b.dma("g", dS[os2][:], dTs[o * 128:(o + 1) * 128, :],
                                  reads=[("dS_c", os2)], writes=[("dS", os2)],
                                  sem=f"dS{os2}")
                            for ic in range(NIC):
                                def mmd2(eng, os2=os2, ic=ic):
                                    return eng.matmul(
                                        p_d[os2][:],
                                        dS[os2][:, ic * 128:(ic + 1) * 128],
                                        hw[:, ic, :],
                                        start=(ic == 0), stop=(ic == NIC - 1))
                                b.op("t", mmd2, reads=[("dS", os2), ("hw", ic)],
                                     writes=[("p6d", os2)])
                            DVE(lambda eng, os2=os2, o=o, tsl=tsl: eng.tensor_add(
                                    x2pre[:, o, :], p_d[os2][:],
                                    x1buf[:, o, tsl]),
                                reads=[("p6d", os2), ("x1", o, th)],
                                writes=[("x2pre", o), ("dS_c", os2)])
                            ACT(lambda eng, o=o: eng.square(
                                    out=sq6[:], in_=x2pre[:, o, :]),
                                reads=[("x2pre", o)], writes=[("sq6",)])
                            def mmsx6(eng, o=o):
                                return eng.matmul(p_s6[0][:], onesb[:],
                                                  x2pre[:, o, :],
                                                  start=(o == 0),
                                                  stop=(o == KCH - 1))
                            b.op("t", mmsx6, reads=[("x2pre", o), ("onesb",)],
                                 writes=[("p6sx",)])
                            def mmsq6(eng, o=o):
                                return eng.matmul(p_s6[1][:], onesb[:], sq6[:],
                                                  start=(o == 0),
                                                  stop=(o == KCH - 1))
                            b.op("t", mmsq6, reads=[("sq6",), ("onesb",)],
                                 writes=[("p6sq",)])
                        ACT(lambda eng: eng.activation(out=st6_m[:], in_=p_s6[0][:],
                                                       func=AF.Copy, scale=1.0 / D),
                            reads=[("p6sx",)], writes=[("st6m",)])
                        ACT(lambda eng: eng.activation(out=st6_v[:], in_=p_s6[1][:],
                                                       func=AF.Copy, scale=1.0 / D),
                            reads=[("p6sq",)], writes=[("st6v",)])
                        ACT(lambda eng: eng.square(out=st6_l[:], in_=st6_m[:]),
                            reads=[("st6m",)], writes=[("st6l",)])
                        DVE(lambda eng: eng.tensor_sub(st6_v[:], st6_v[:],
                                                       st6_l[:]),
                            reads=[("st6v",), ("st6l",)], writes=[("st6v",)])
                        ACT(lambda eng: eng.activation(out=st6_l[:], in_=st6_v[:],
                                                       func=AF.Ln,
                                                       bias=eps_t[0:1, 1:2]),
                            reads=[("st6v",), ("eps",)], writes=[("st6l",)])
                        ACT(lambda eng: eng.activation(out=rstd6[:], in_=st6_l[:],
                                                       func=AF.Exp, scale=-0.5),
                            reads=[("st6l",)], writes=[("rstd6",)])
                        DVE(lambda eng: eng.tensor_mul(st6_m[:], st6_m[:],
                                                       rstd6[:]),
                            reads=[("st6m",), ("rstd6",)], writes=[("st6m",)])
                        DVE(lambda eng: eng.tensor_scalar_mul(nmrs6[:], st6_m[:],
                                                              -1.0),
                            reads=[("st6m",)], writes=[("nmrs6",)])
                        def mb6a(eng):
                            return eng.matmul(p_b6[:], onesk1b[:], rstd6[:],
                                              start=True, stop=True)
                        b.op("t", mb6a, reads=[("rstd6",), ("onesk1b",)],
                             writes=[("p6bA",)])
                        ACT(lambda eng: eng.copy(out=A6[:], in_=p_b6[:]),
                            reads=[("p6bA",)], writes=[("A6",)])
                        def mb6b(eng):
                            return eng.matmul(p_b6[:], onesk1b[:], nmrs6[:],
                                              start=True, stop=True)
                        b.op("t", mb6b, reads=[("nmrs6",), ("onesk1b",), ("A6",)],
                             writes=[("p6bB",)])
                        ACT(lambda eng: eng.copy(out=B6[:], in_=p_b6[:]),
                            reads=[("p6bB",)], writes=[("B6",)])
                        for o in range(KCH):
                            DVE(lambda eng, o=o: eng.tensor_mul(
                                    outsb[:], x2pre[:, o, :], A6[:]),
                                reads=[("x2pre", o), ("A6",)], writes=[("outsb",)])
                            DVE(lambda eng: eng.tensor_add(
                                    outsb[:], outsb[:], B6[:]),
                                reads=[("outsb",), ("B6",)], writes=[("outsb",)])
                            b.dma("q", outT[o * 128:(o + 1) * 128, tsl], outsb[:],
                                  reads=[("outsb",)], writes=[("outT",)],
                                  sem="out")
                b.barrier()

        with nc.Block() as block:
            b.finalize_and_emit(block, sems)
    return nc


def stage_inputs(inputs, S):
    T = S // NCORES
    NSUB = T // 64
    x = np.asarray(inputs["L0_attn_x"], np.float32).reshape(S, D)
    wq_a = np.asarray(inputs["L0_attn_wq_a"], np.float32)
    q_ln = np.asarray(inputs["L0_attn_q_a_ln_w"], np.float32)
    wq_b = np.asarray(inputs["L0_attn_wq_b"], np.float32)
    wkv_a = np.asarray(inputs["L0_attn_wkv_a"], np.float32)
    kv_ln = np.asarray(inputs["L0_attn_kv_a_ln_w"], np.float32)
    wkv_b = np.asarray(inputs["L0_attn_wkv_b"], np.float32)
    wo = np.asarray(inputs["L0_attn_wo"], np.float32)
    gate_w = np.asarray(inputs["L0_ffn_gate_w"], np.float32)
    u = np.asarray(inputs["L0_ffn_u"], np.float32)
    g = np.asarray(inputs["L0_ffn_g"], np.float32)
    d = np.asarray(inputs["L0_ffn_d"], np.float32)

    xT = np.ascontiguousarray(x.T)
    wqaT = np.ascontiguousarray(
        wq_a.T.reshape(KCH, 128, DQA // 128, 128).transpose(2, 1, 0, 3)).reshape(
        DQA // 128 * 128, D)
    wqb = wq_b.T.reshape(DQA, H, QHD)
    wqb2 = np.concatenate([wqb[:, :, NOPE:], wqb[:, :, :NOPE]], axis=2)
    wqbT = np.ascontiguousarray(wqb2.reshape(DQA, D)).reshape(DQA // 128, 128, D)
    wkvaT = np.ascontiguousarray(wkv_a.T).reshape(KCH, 128, KVR + ROPE)
    wkvbT = np.ascontiguousarray(wkv_b.T).reshape(KVR // 128, 128, H * (NOPE + VD))
    woTs = np.ascontiguousarray(
        wo.T.reshape(KCH, 128, KCH, 128).transpose(2, 1, 0, 3)).reshape(KCH * 128, D)
    uTs = np.ascontiguousarray(
        u.T.reshape(KCH, 128, NIC, 128).transpose(2, 1, 0, 3)).reshape(NIC * 128, D)
    gTs = np.ascontiguousarray(
        g.T.reshape(KCH, 128, NIC, 128).transpose(2, 1, 0, 3)).reshape(NIC * 128, D)
    dTs = np.ascontiguousarray(
        d.T.reshape(NIC, 128, KCH, 128).transpose(2, 1, 0, 3)).reshape(KCH * 128, DI)
    gwT = np.ascontiguousarray(gate_w.T).reshape(KCH, 128, NE)
    qlnw = np.ascontiguousarray(q_ln.reshape(DQA // 128, 128).T)
    kvlnw = np.ascontiguousarray(kv_ln.reshape(KVR // 128, 128).T)
    cosf, sinf = yarn_cos_sin(S)
    coskT = np.ascontiguousarray(cosf.T)
    sinkT = np.ascontiguousarray(sinf.T)

    shared = dict(xT=xT, wqaT=wqaT, wqbT=wqbT, wkvaT=wkvaT, wkvbT=wkvbT,
                  woTs=woTs, uTs=uTs, gTs=gTs, dTs=dTs, gwT=gwT,
                  qlnw=qlnw, kvlnw=kvlnw, coskT=coskT, sinkT=sinkT)
    in_maps = []
    tok_idx_all = []
    for c in range(NCORES):
        toks = np.concatenate([np.arange(64) + (8 * j + c) * 64
                               for j in range(NSUB)])
        tok_idx_all.append(toks)
        r = np.arange(128) % 64
        bound = 64 * c + r
        kk = np.arange(512)[None, :]
        mA = np.where(kk <= bound[:, None], 0.0, NEG).astype(np.float32)
        mA[64:, :] = 0.0
        mB = np.where(kk <= bound[:, None], 0.0, NEG).astype(np.float32)
        mB[:64, :] = NEG
        mA = mA.astype(BF)
        mB = mB.astype(BF)
        m = dict(shared)
        m["xqT"] = np.ascontiguousarray(xT[:, toks])
        m["cosqT"] = np.ascontiguousarray(coskT[:, toks])
        m["sinqT"] = np.ascontiguousarray(sinkT[:, toks])
        m["maskA"] = mA
        m["maskB"] = mB
        in_maps.append(m)
    return in_maps, tok_idx_all


def kernel(**inputs):
    x = np.asarray(inputs["L0_attn_x"])
    S = x.shape[1]
    nc = build_program(S)
    in_maps, tok_idx = stage_inputs(inputs, S)
    res = run_bass_kernel_spmd(nc, in_maps, core_ids=list(range(NCORES)))
    out = np.empty((S, D), np.float32)
    for c in range(NCORES):
        out[tok_idx[c], :] = np.asarray(res.results[c]["outT"]).T
    return out.reshape(1, S, D)


# version 15 (final)
# speedup vs baseline: 1.0294x; 1.0294x over previous
import math
import numpy as np
import ml_dtypes
from contextlib import ExitStack

import concourse.bass as bass
import concourse.mybir as mybir
from concourse.bass_utils import run_bass_kernel_spmd

F32 = mybir.dt.float32
BF16 = mybir.dt.bfloat16
AF = mybir.ActivationFunctionType
ALU = mybir.AluOpType
AX = mybir.AxisListType.X
BF = ml_dtypes.bfloat16

NCORES = 8
H = 32
ROPE = 64
NOPE = 64
VD = 128
QHD = 128
KVR = 256
D = 4096
DQA = 1024
NE = 16
INTER = 448
DI = NE * INTER
KCH = D // 128
NIC = DI // 128
EPS_RMS = 1e-6
EPS_LN = 1e-5
_m = 0.1 * 1.0 * math.log(128.0) + 1.0
SCALE = float(QHD ** (-0.5) * _m * _m)
NEG = -3.0e38


def yarn_cos_sin(seq_len, dim=ROPE, base=10000.0, factor=128.0, beta_fast=32.0,
                 beta_slow=1.0, orig_max=8192, mscale=1.0, mscale_all_dim=1.0):
    def corr_dim(r):
        return dim * math.log(orig_max / (r * 2 * math.pi)) / (2 * math.log(base))
    low = max(math.floor(corr_dim(beta_fast)), 0)
    high = min(math.ceil(corr_dim(beta_slow)), dim - 1)
    hi = high + 0.001 if low == high else float(high)
    ramp = np.clip((np.arange(dim // 2, dtype=np.float32) - low) / (hi - low), 0.0, 1.0)
    inv_freq_mask = 1.0 - ramp
    freq_extra = 1.0 / base ** (np.arange(0, dim, 2, dtype=np.float32) / dim)
    freq_inter = freq_extra / factor
    inv_freq = freq_inter * (1.0 - inv_freq_mask) + freq_extra * inv_freq_mask
    t = np.arange(seq_len, dtype=np.float32)
    freqs = np.outer(t, inv_freq)
    emb = np.concatenate([freqs, freqs], axis=-1)
    def gm(s, m):
        return 1.0 if s <= 1 else 0.1 * m * math.log(s) + 1.0
    ms = gm(factor, mscale) / gm(factor, mscale_all_dim)
    return (np.cos(emb) * ms).astype(np.float32), (np.sin(emb) * ms).astype(np.float32)


def rope_perms():
    d1 = np.zeros((ROPE, ROPE), np.float32)
    d2 = np.zeros((ROPE, ROPE), np.float32)
    for i in range(ROPE // 2):
        d1[i, 2 * i] = 1.0
        d1[ROPE // 2 + i, 2 * i + 1] = 1.0
        d2[i, 2 * i + 1] = -1.0
        d2[ROPE // 2 + i, 2 * i] = 1.0
    return d1, d2


class Ms:
    __slots__ = ("eng", "idx", "dma", "queue")
    def __init__(self, eng, idx, dma=False):
        self.eng = eng
        self.idx = idx
        self.dma = dma


class Op:
    __slots__ = ("emit", "deps", "inc_needed", "dma", "val", "semname")
    def __init__(self, emit, dma):
        self.emit = emit
        self.deps = []
        self.inc_needed = False
        self.dma = dma
        self.val = None


class Builder:
    def __init__(self, nc, mk_sem):
        self.nc = nc
        self.mk_sem = mk_sem
        self.ops = {e: [] for e in ("t", "v", "s", "g", "q")}
        self.dma_ops = []
        self.dma_count = {}
        self.dma_last = {}
        self.lastw = {}
        self.lastr = {}
        self.last_cms = {}
        self.bar_n = 0

    def _deps_for(self, reads, writes):
        deps = []
        for k in reads:
            for m in self.lastw.get(k, {}).values():
                deps.append(m)
        for k in writes:
            for m in self.lastw.get(k, {}).values():
                deps.append(m)
            for m in self.lastr.get(k, {}).values():
                deps.append(m)
        return deps

    def _update_rw(self, key_ms, reads, writes):
        for k in writes:
            self.lastw.setdefault(k, {})[key_ms.eng if not key_ms.dma else ("d", key_ms.eng)] = key_ms
            self.lastr[k] = {}
        for k in reads:
            self.lastr.setdefault(k, {})[key_ms.eng if not key_ms.dma else ("dr", key_ms.eng, key_ms.idx)] = key_ms

    def op(self, eng, emit, reads=(), writes=(), extra_deps=()):
        reads = tuple(reads)
        writes = tuple(writes)
        o = Op(emit, False)
        deps = list(extra_deps) + self._deps_for(reads, writes)
        ms = Ms(eng, len(self.ops[eng]), False)
        self._update_rw(ms, reads, writes)
        o.deps = [d for d in deps
                  if not (eng == "t" and not d.dma and d.eng == "t")]
        self.ops[eng].append(o)
        self.last_cms[eng] = ms
        return ms

    def dma(self, queue, out_ap, in_ap, reads=(), writes=(), sem="cst", **kw):
        sem = f"{queue}_{sem}"
        reads = tuple(reads)
        writes = tuple(writes)
        def emit(eng, _o=out_ap, _i=in_ap, _kw=kw):
            return eng.dma_start(out=_o, in_=_i, **_kw)
        o = Op(emit, True)
        o.inc_needed = True
        deps = self._deps_for(reads, writes)
        prev = self.dma_last.get(sem)
        if prev is not None:
            deps.append(prev)
        cnt = self.dma_count.get(sem, 0) + 16
        self.dma_count[sem] = cnt
        ms = Ms(sem, cnt, True)
        ms.queue = queue
        o.val = cnt
        o.deps = deps
        self._update_rw(ms, reads, writes)
        self.dma_last[sem] = ms
        self.ops[queue].append(o)
        self.dma_ops.append((sem, o, queue))
        return ms

    def barrier(self):
        self.bar_n += 1
        wkeys = []
        for e in ("t", "v", "s", "g"):
            k = ("__barw__", self.bar_n, e)
            extra = [self.last_cms[e]] if e in self.last_cms else []
            self.op(e, lambda eng: None, reads=(), writes=(k,), extra_deps=extra)
            wkeys.append(k)
        extra = list(self.dma_last.values())
        for e in ("t", "v", "s", "g", "q"):
            self.op(e, lambda eng: None, reads=tuple(wkeys), writes=(),
                    extra_deps=extra)

    def finalize_and_emit(self, block, sems):
        for elist in self.ops.values():
            for o in elist:
                for d in o.deps:
                    if not d.dma:
                        self.ops[d.eng][d.idx].inc_needed = True
        for e, elist in self.ops.items():
            cc = 0
            for o in elist:
                if o.dma:
                    continue
                if o.inc_needed:
                    cc += 1
                o.val = cc
        dsem_handles = {name: self.mk_sem(f"dm_{name}")
                        for name in self.dma_count}
        bld = self

        def emit_for(engkey, engobj):
            hwm = {}
            for o in bld.ops[engkey]:
                need = {}
                for d in o.deps:
                    if d.dma:
                        skey = ("dma", d.eng)
                        v = d.idx
                    else:
                        dep = bld.ops[d.eng][d.idx]
                        if not dep.inc_needed or dep.val == 0:
                            continue
                        skey = ("c", d.eng)
                        v = dep.val
                    if need.get(skey, -1) < v:
                        need[skey] = v
                for skey in sorted(need):
                    v = need[skey]
                    if hwm.get(skey, 0) >= v:
                        continue
                    sem = dsem_handles[skey[1]] if skey[0] == "dma" else sems[skey[1]]
                    engobj.wait_ge(sem, v)
                    hwm[skey] = v
                inst = o.emit(engobj)
                if o.inc_needed:
                    if o.dma:
                        inst.then_inc(dsem_handles[o.semname], 16)
                    else:
                        if inst is None:
                            skey = ("c", engkey)
                            pv = o.val - 1
                            if pv > 0 and hwm.get(skey, 0) < pv:
                                engobj.wait_ge(sems[engkey], pv)
                                hwm[skey] = pv
                            inst = engobj.nop()
                        inst.then_inc(sems[engkey], 1)

        for sem, o, queue in self.dma_ops:
            o.semname = sem

        @block.sync
        def _(sync):
            emit_for("q", sync)
            for name, tot in sorted(bld.dma_count.items()):
                sync.wait_ge(dsem_handles[name], tot)

        @block.tensor
        def _(tensor):
            emit_for("t", tensor)

        @block.vector
        def _(vector):
            emit_for("v", vector)

        @block.scalar
        def _(scalar):
            emit_for("s", scalar)

        @block.gpsimd
        def _(gp):
            emit_for("g", gp)


def build_program(S):
    T = S // NCORES
    NSUB = T // 64
    NP = NSUB // 2
    NKT = S // 512
    TT = T // 512
    assert NSUB * 64 == T and NP * 2 == NSUB and NKT * 512 == S and TT * 512 == T

    nc = bass.Bass("TRN2", target_bir_lowering=False)

    def din(name, shape, dt=F32):
        return nc.dram_tensor(name, list(shape), dt, kind="ExternalInput").ap()

    xT = din("xT", (D, S))
    xqT = din("xqT", (D, T))
    wqaT = din("wqaT", (DQA // 128 * 128, D))
    wqbT = din("wqbT", (DQA // 128, 128, D))
    wkvaT = din("wkvaT", (KCH, 128, KVR + ROPE))
    wkvbT = din("wkvbT", (KVR // 128, 128, H * (NOPE + VD)))
    woTs = din("woTs", (KCH * 128, D))
    uTs = din("uTs", (NIC * 128, D))
    gTs = din("gTs", (NIC * 128, D))
    dTs = din("dTs", (KCH * 128, DI))
    gwT = din("gwT", (KCH, 128, NE))
    qlnw = din("qlnw", (128, DQA // 128))
    kvlnw = din("kvlnw", (128, KVR // 128))
    coskT = din("coskT", (ROPE, S))
    sinkT = din("sinkT", (ROPE, S))
    cosqT = din("cosqT", (ROPE, T))
    sinqT = din("sinqT", (ROPE, T))
    maskA = din("maskA", (128, 512), BF16)
    maskB = din("maskB", (128, 512), BF16)

    outT = nc.dram_tensor("outT", [D, T], F32, kind="ExternalOutput").ap()

    attT_sc = nc.dram_tensor("attT_sc", [D, T], BF16).ap()

    id128b_c = nc.inline_tensor(np.eye(128, dtype=BF), name="id128b").ap()
    id128f_c = nc.inline_tensor(np.eye(128, dtype=np.float32), name="id128f").ap()
    id16f_c = nc.inline_tensor(np.eye(16, dtype=np.float32), name="id16f").ap()
    d1np, d2np = rope_perms()
    d1T_c = nc.inline_tensor(np.ascontiguousarray(d1np.T).astype(BF), name="d1T").ap()
    d2T_c = nc.inline_tensor(np.ascontiguousarray(d2np.T).astype(BF), name="d2T").ap()
    ones128_c = nc.inline_tensor(np.ones((128, 1), dtype=BF), name="ones128").ap()
    onesk1_c = nc.inline_tensor(np.ones((1, 128), dtype=BF), name="onesk1").ap()
    esel_np = np.zeros((NE, DI), dtype=np.float32)
    for e in range(NE):
        esel_np[e, e * INTER:(e + 1) * INTER] = 1.0
    esel_c = nc.inline_tensor(esel_np.astype(BF), name="esel").ap()

    with ExitStack() as es:
        sems = {e: es.enter_context(nc.semaphore(f"p_{e}")) for e in
                ("t", "v", "s", "g", "q")}
        _semcache = {}

        def mk_sem(name):
            if name not in _semcache:
                _semcache[name] = es.enter_context(nc.semaphore(name))
            return _semcache[name]

        b = Builder(nc, mk_sem)

        _uniq = [0]

        import os as _os
        def sb(stack, name, shape, dt):
            _uniq[0] += 1
            t = stack.enter_context(
                nc.sbuf_tensor(f"{name}_{_uniq[0]}", list(shape), dt))
            if _os.environ.get("SBUF_DEBUG"):
                print(f"alloc {name:12s} {str(shape):24s} remaining={nc.sbuf_bytes_remaining}")
            return t

        def psb(stack, name, shape, dt=F32):
            _uniq[0] += 1
            return stack.enter_context(
                nc.psum_tensor(f"{name}_{_uniq[0]}", list(shape), dt))

        eps_t = sb(es, "eps_t", (1, 2), F32)
        b.op("g", lambda eng: eng.memset(eps_t[0:1, 0:1], EPS_RMS), writes=[("eps",)])
        b.op("g", lambda eng: eng.memset(eps_t[0:1, 1:2], EPS_LN), writes=[("eps",)])
        idb = sb(es, "idb", (128, 128), BF16)
        onesb = sb(es, "onesb", (128, 1), BF16)
        onesk1b = sb(es, "onesk1b", (1, 128), BF16)
        b.dma("q", idb[:], id128b_c, writes=[("idb",)])
        b.dma("q", onesb[:], ones128_c, writes=[("onesb",)])
        b.dma("q", onesk1b[:], onesk1_c, writes=[("onesk1b",)])

        def ACT(emit, reads, writes):
            return b.op("s", emit, reads, writes)

        def DVE(emit, reads, writes):
            return b.op("v", emit, reads, writes)

        with ExitStack() as eA:
            ckv = [sb(eA, f"ckv{r}", (128, S), BF16) for r in range(2)]
            kfull = [sb(eA, f"kfull{i}", (128, S), BF16) for i in range(2)]
            qan = sb(eA, "qan", (128, DQA // 128, T), BF16)
            d1b = sb(eA, "d1b", (ROPE, ROPE), BF16)
            d2b = sb(eA, "d2b", (ROPE, ROPE), BF16)
            b.dma("q", d1b[:], d1T_c, writes=[("d1b",)])
            b.dma("q", d2b[:], d2T_c, writes=[("d2b",)])

            SEG = min(1024, S)
            NSEG = S // SEG
            ST = SEG // 512
            with ExitStack() as e1:
                xseg = sb(e1, "xseg", (128, KCH, SEG), BF16)
                wkva_sb = sb(e1, "wkva_sb", (128, KCH, KVR + ROPE), BF16)
                ckvf = sb(e1, "ckvf", (128, 2, SEG), F32)
                kpeb = sb(e1, "kpeb", (ROPE, SEG), BF16)
                sqv = sb(e1, "sqv", (128, SEG), BF16)
                rstd_b = sb(e1, "rstd_b", (1, SEG), BF16)
                ln_tmp = sb(e1, "ln_tmp", (1, SEG), F32)
                ropek = sb(e1, "ropek", (ROPE, 2, SEG), F32)
                tmpr = sb(e1, "tmpr", (ROPE, SEG), F32)
                kvlnw_sb = sb(e1, "kvlnw_sb", (128, KVR // 128), F32)
                p_acc = [psb(e1, f"p1a{i}", (128, 512)) for i in range(3)]
                p_red = [psb(e1, f"p1r{i}", (1, 512)) for i in range(ST)]
                p_bc = psb(e1, "p1bc", (128, 512))
                p_rope = [psb(e1, f"p1ro{i}", (ROPE, 512)) for i in range(2)]

                b.dma("g", wkva_sb[:], wkvaT.rearrange("c p o -> p c o"),
                      writes=[("wkva",)])
                b.dma("q", kvlnw_sb[:], kvlnw, writes=[("kvlnw",)])

                for sg in range(NSEG):
                    s0 = sg * SEG
                    for c in range(KCH):
                        b.dma("g", xseg[:, c, :],
                              xT[c * 128:(c + 1) * 128, s0:s0 + SEG],
                              writes=[("xseg", c)], sem=f"xs{c % 4}")
                    b.dma("q", ropek[:, 0, :], coskT[:, s0:s0 + SEG],
                          writes=[("ropek",)], sem="ropek")
                    b.dma("q", ropek[:, 1, :], sinkT[:, s0:s0 + SEG],
                          writes=[("ropek",)], sem="ropek")
                    for tt in range(ST):
                        for o in range(3):
                            mwid = 128 if o < 2 else ROPE
                            for c in range(KCH):
                                def mm(eng, c=c, o=o, tt=tt, mwid=mwid):
                                    return eng.matmul(
                                        p_acc[o][0:mwid, :],
                                        wkva_sb[:, c, o * 128:o * 128 + mwid],
                                        xseg[:, c, tt * 512:(tt + 1) * 512],
                                        start=(c == 0), stop=(c == KCH - 1))
                                b.op("t", mm, reads=[("xseg", c), ("wkva",)],
                                     writes=[("p1a", o)])
                            if o < 2:
                                ACT(lambda eng, o=o, tt=tt: eng.copy(
                                        out=ckvf[:, o, tt * 512:(tt + 1) * 512],
                                        in_=p_acc[o][:]),
                                    reads=[("p1a", o)], writes=[("ckvf", o, tt)])
                            else:
                                ACT(lambda eng, tt=tt: eng.copy(
                                        out=kpeb[:, tt * 512:(tt + 1) * 512],
                                        in_=p_acc[2][0:ROPE, :]),
                                    reads=[("p1a", 2)], writes=[("kpeb", tt)])
                    for tt in range(ST):
                        for o in range(2):
                            ACT(lambda eng, o=o, tt=tt: eng.square(
                                    out=sqv[:, tt * 512:(tt + 1) * 512],
                                    in_=ckvf[:, o, tt * 512:(tt + 1) * 512]),
                                reads=[("ckvf", o, tt)], writes=[("sqv", tt)])
                            def mm(eng, o=o, tt=tt):
                                return eng.matmul(
                                    p_red[tt][:], onesb[:],
                                    sqv[:, tt * 512:(tt + 1) * 512],
                                    start=(o == 0), stop=(o == 1))
                            b.op("t", mm, reads=[("sqv", tt), ("onesb",)],
                                 writes=[("p1r", tt)])
                        ACT(lambda eng, tt=tt: eng.activation(
                                out=ln_tmp[:, tt * 512:(tt + 1) * 512],
                                in_=p_red[tt][:],
                                func=AF.Ln, bias=eps_t[0:1, 0:1], scale=1.0 / KVR),
                            reads=[("p1r", tt), ("eps",)], writes=[("lnt", tt)])
                        ACT(lambda eng, tt=tt: eng.activation(
                                out=rstd_b[:, tt * 512:(tt + 1) * 512],
                                in_=ln_tmp[:, tt * 512:(tt + 1) * 512],
                                func=AF.Exp, scale=-0.5),
                            reads=[("lnt", tt)], writes=[("rstd", tt)])
                        def mmb(eng, tt=tt):
                            return eng.matmul(
                                p_bc[:], onesk1b[:],
                                rstd_b[:, tt * 512:(tt + 1) * 512],
                                start=True, stop=True)
                        b.op("t", mmb, reads=[("rstd", tt), ("onesk1b",)],
                             writes=[("p1bc",)])
                        for o in range(2):
                            DVE(lambda eng, o=o, tt=tt, t0=s0 + tt * 512:
                                eng.scalar_tensor_tensor(
                                    out=ckv[o][:, t0:t0 + 512],
                                    in0=ckvf[:, o, tt * 512:(tt + 1) * 512],
                                    scalar=kvlnw_sb[:, o:o + 1],
                                    in1=p_bc[:],
                                    op0=ALU.mult, op1=ALU.mult),
                                reads=[("ckvf", o, tt), ("p1bc",), ("kvlnw",)],
                                writes=[("ckvn",)])
                        for pi, dbuf in enumerate((d1b, d2b)):
                            def mmr(eng, pi=pi, dbuf=dbuf, tt=tt):
                                return eng.matmul(
                                    p_rope[pi][:], dbuf[:],
                                    kpeb[:, tt * 512:(tt + 1) * 512],
                                    start=True, stop=True)
                            b.op("t", mmr,
                                 reads=[("kpeb", tt), ("d1b",), ("d2b",)],
                                 writes=[("p1ro", pi)])
                        DVE(lambda eng, tt=tt: eng.tensor_mul(
                                tmpr[:, tt * 512:(tt + 1) * 512], p_rope[0][:],
                                ropek[:, 0, tt * 512:(tt + 1) * 512]),
                            reads=[("p1ro", 0), ("ropek",)], writes=[("tmpr", tt)])
                        DVE(lambda eng, tt=tt: eng.tensor_mul(
                                p_rope[1][:], p_rope[1][:],
                                ropek[:, 1, tt * 512:(tt + 1) * 512]),
                            reads=[("p1ro", 1), ("ropek",)], writes=[("p1ro", 1)])
                        for kf in range(2):
                            DVE(lambda eng, tt=tt, t0=s0 + tt * 512, kf=kf:
                                eng.tensor_add(
                                    kfull[kf][0:ROPE, t0:t0 + 512],
                                    tmpr[:, tt * 512:(tt + 1) * 512],
                                    p_rope[1][:]),
                                reads=[("tmpr", tt), ("p1ro", 1)],
                                writes=[("kfull_pe", kf)])
            b.barrier()

            with ExitStack() as e2:
                xq = sb(e2, "xq", (128, KCH, T), BF16)
                qaS = [sb(e2, f"qaS{i}", (128, D), BF16) for i in range(2)]
                qaf = sb(e2, "qaf", (128, DQA // 128, T), F32)
                sqq = sb(e2, "sqq", (128, T), BF16)
                rstdq = sb(e2, "rstdq", (1, T), BF16)
                lnq_tmp = sb(e2, "lnq_tmp", (1, T), F32)
                qlnw_sb = sb(e2, "qlnw_sb", (128, DQA // 128), F32)
                p_qa = [psb(e2, f"p2a{i}", (128, 512)) for i in range(2)]
                p_qr = [psb(e2, f"p2r{i}", (1, 512)) for i in range(TT)]
                p_qbc = psb(e2, "p2bc", (128, 512))

                b.dma("q", qlnw_sb[:], qlnw, writes=[("qlnw",)])
                for c in range(KCH):
                    b.dma("g", xq[:, c, :], xqT[c * 128:(c + 1) * 128, :],
                          writes=[("xq", c)], sem=f"xq{c % 4}")
                NO8 = DQA // 128
                for o in range(NO8):
                    b.dma("g", qaS[o % 2][:], wqaT[o * 128:(o + 1) * 128, :],
                          reads=[("qaS_c", o % 2)], writes=[("qaS", o % 2)],
                          sem=f"qaS{o % 2}")
                    for tt in range(TT):
                        for c in range(KCH):
                            def mm(eng, o=o, tt=tt, c=c):
                                return eng.matmul(
                                    p_qa[o % 2][:],
                                    qaS[o % 2][:, c * 128:(c + 1) * 128],
                                    xq[:, c, tt * 512:(tt + 1) * 512],
                                    start=(c == 0), stop=(c == KCH - 1))
                            b.op("t", mm, reads=[("xq", c), ("qaS", o % 2)],
                                 writes=[("p2a", o % 2)])
                        ACT(lambda eng, o=o, tt=tt: eng.copy(
                                out=qaf[:, o, tt * 512:(tt + 1) * 512],
                                in_=p_qa[o % 2][:]),
                            reads=[("p2a", o % 2)], writes=[("qaf", o, tt), ("qaS_c", o % 2)])
                        ACT(lambda eng, o=o, tt=tt: eng.square(
                                out=sqq[:, tt * 512:(tt + 1) * 512],
                                in_=qaf[:, o, tt * 512:(tt + 1) * 512]),
                            reads=[("qaf", o, tt)], writes=[("sqq", tt)])
                        def mm2(eng, o=o, tt=tt):
                            return eng.matmul(
                                p_qr[tt][:], onesb[:],
                                sqq[:, tt * 512:(tt + 1) * 512],
                                start=(o == 0), stop=(o == NO8 - 1))
                        b.op("t", mm2, reads=[("sqq", tt), ("onesb",)],
                             writes=[("p2r", tt)])
                for tt in range(TT):
                    ACT(lambda eng, tt=tt: eng.activation(
                            out=lnq_tmp[:, tt * 512:(tt + 1) * 512], in_=p_qr[tt][:],
                            func=AF.Ln, bias=eps_t[0:1, 0:1], scale=1.0 / DQA),
                        reads=[("p2r", tt), ("eps",)], writes=[("lnq", tt)])
                    ACT(lambda eng, tt=tt: eng.activation(
                            out=rstdq[:, tt * 512:(tt + 1) * 512],
                            in_=lnq_tmp[:, tt * 512:(tt + 1) * 512],
                            func=AF.Exp, scale=-0.5),
                        reads=[("lnq", tt)], writes=[("rstdq", tt)])
                    def mmb(eng, tt=tt):
                        return eng.matmul(p_qbc[:], onesk1b[:],
                                          rstdq[:, tt * 512:(tt + 1) * 512],
                                          start=True, stop=True)
                    b.op("t", mmb, reads=[("rstdq", tt), ("onesk1b",)],
                         writes=[("p2bc",)])
                    for o in range(NO8):
                        DVE(lambda eng, o=o, tt=tt: eng.scalar_tensor_tensor(
                                out=qan[:, o, tt * 512:(tt + 1) * 512],
                                in0=qaf[:, o, tt * 512:(tt + 1) * 512],
                                scalar=qlnw_sb[:, o:o + 1],
                                in1=p_qbc[:],
                                op0=ALU.mult, op1=ALU.mult),
                            reads=[("qaf", o, tt), ("p2bc",), ("qlnw",)],
                            writes=[("qan",)])
            b.barrier()

            with ExitStack() as e3:
                vbuf = sb(e3, "vbuf", (128, NKT * 4, 512), BF16)
                attnb = sb(e3, "attn0", (128, NKT * 512), BF16)
                attnT = sb(e3, "attnT", (128, 16, 128), BF16)
                qh = sb(e3, "qh0", (128, T), BF16)
                qpe = sb(e3, "qpe", (ROPE, T), BF16)
                wqb_h = [sb(e3, "wqbh0", (128, DQA // 128, 128), BF16)]
                wkn = [sb(e3, f"wkn{i}", (128, KVR // 128, NOPE), BF16)
                       for i in range(2)]
                wkv = [sb(e3, f"wkv{i}", (128, KVR // 128, 512), BF16)
                       for i in range(2)]
                ropeq = sb(e3, "ropeq", (ROPE, 2, T), BF16)
                tmpq = sb(e3, "tmpq", (ROPE, 512), BF16)
                maskA_sb = sb(e3, "maskA_sb", (128, 512), BF16)
                maskB_sb = sb(e3, "maskB_sb", (128, 512), BF16)
                mx = sb(e3, "mx", (128, max(NKT, 2)), F32)
                rs = sb(e3, "rs", (128, 4, max(NKT + 1, 2)), F32)
                nmax = [sb(e3, f"nmax{i}", (128, 1), F32) for i in range(2)]
                biasq = [sb(e3, f"biasq{i}", (128, 1), F32) for i in range(2)]
                rsum = sb(e3, "rsum", (128, 1), F32)
                rcpc = sb(e3, "rcpc", (128, 1), BF16)
                rrowb = sb(e3, "rrowb", (1, 128), BF16)
                rnorm = sb(e3, "rnorm", (128, 512), BF16)
                avsb = sb(e3, "avsb", (128, 512), BF16)
                p_sc4 = psb(e3, "p3s4", (128, 2048))
                p_sc = [p_sc4.ap()[:, i * 512:(i + 1) * 512] for i in range(4)]
                p_dg = [psb(e3, "p3d0", (128, 512), BF16)]
                p_av = psb(e3, "p3av", (128, 512))
                p_dec = [psb(e3, f"p3k{i}", (128, 512)) for i in range(2)]

                b.dma("q", maskA_sb[:], maskA, writes=[("maskA",)])
                b.dma("q", maskB_sb[:], maskB, writes=[("maskB",)])
                b.dma("g", ropeq[:, 0, :], cosqT, writes=[("ropeq",)])
                b.dma("g", ropeq[:, 1, :], sinqT, writes=[("ropeq",)])

                wkvbT_r = wkvbT.rearrange("c p (hh x) -> p c hh x", x=NOPE + VD)
                NO8 = DQA // 128
                PGRP = 4 if NP >= 4 else NP
                pgroups = [list(range(g, min(g + PGRP, NP)))
                           for g in range(0, NP, PGRP)]
                scb = [0]

                class Pending:
                    def __init__(self, h, hl, hp, m, pl, last_in_grp, grp, bA):
                        par = m % 2
                        DVE(lambda eng, par=par: eng.reduce_max(
                                nmax[par][:], mx[:, 0:2], axis=AX,
                                negate=True),
                            reads=[("mx",)], writes=[("nmax", par)])
                        DVE(lambda eng, par=par: eng.tensor_scalar_mul(
                                biasq[par][:], nmax[par][:], SCALE),
                            reads=[("nmax", par)], writes=[("biasq", par)])

                        def rowsum_ops(m=m, pl=pl):
                            DVE(lambda eng, m=m, pl=pl: eng.reduce_sum(
                                    rsum[:], rs[:, pl, 0:m + 1], axis=AX),
                                reads=[("rs", pl)], writes=[("rsum",)])
                            DVE(lambda eng: eng.reciprocal(rsum[:], rsum[:]),
                                reads=[("rsum",)], writes=[("rsum",)])
                            DVE(lambda eng: eng.tensor_copy(out=rcpc[:], in_=rsum[:]),
                                reads=[("rsum",)], writes=[("rcpc",)])
                            def ttr(eng):
                                return eng.transpose(p_dg[0][0:1, 0:128], rcpc[:],
                                                     idb[:])
                            b.op("t", ttr, reads=[("rcpc",), ("idb",)],
                                 writes=[("p3d", 0)])
                            ACT(lambda eng: eng.copy(out=rrowb[:],
                                                     in_=p_dg[0][0:1, 0:128]),
                                reads=[("p3d", 0)], writes=[("rrowb",)])
                            def tbc(eng):
                                return eng.matmul(p_sc[2][:, 0:128], onesk1b[:],
                                                  rrowb[:], start=True, stop=True)
                            b.op("t", tbc, reads=[("rrowb",), ("onesk1b",)],
                                 writes=[("p3s", 2)])
                            ACT(lambda eng, pl=pl: eng.copy(
                                    out=rnorm[:, pl * 128:(pl + 1) * 128],
                                    in_=p_sc[2][:, 0:128]),
                                reads=[("p3s", 2)], writes=[("rnorm",)])
                        grp_rowsums.append(rowsum_ops)

                        nkj = 2 * m * 4
                        q0 = (2 * m - 2) * 64
                        chunk_t0 = [2 * m - 2] + [2 * c - 2 for c in range(1, m)]
                        kj_pos = []
                        for t0 in chunk_t0:
                            kj_pos += [t0 * 4 + j for j in range(8)]
                        t0d = chunk_t0[0]
                        for d in range(2):
                            ACT(lambda eng, t0d=t0d, d=d, par=par,
                                pl=pl, m=m: eng.activation(
                                    out=attnb[:, (t0d + d) * 512:
                                              (t0d + d + 1) * 512],
                                    in_=p_dec[d][:],
                                    func=AF.Exp, bias=biasq[par][:],
                                    scale=SCALE,
                                    accum_out=rs[:, pl, d * m:d * m + 1]),
                                reads=[("biasq", par), ("p3k", d)],
                                writes=[("attn", t0d + d), ("rs", pl)])
                        self.steps = []
                        for s in range(1, m + 2):
                            def step(s=s, m=m, par=par, pl=pl, hl=hl, hp=hp,
                                     nkj=nkj, h=h, q0=q0, grp=grp, bA=bA,
                                     chunk_t0=chunk_t0, kj_pos=kj_pos,
                                     last=(last_in_grp and s == m + 1)):
                                if s < m:
                                    t0 = chunk_t0[s]
                                    if True:
                                        scb[0] += 2
                                        bank = scb[0] % 4
                                        for d in range(2):
                                            def mmb(eng, t0=t0, d=d, q0=q0,
                                                    hp=hp, bank=bank):
                                                return eng.matmul(
                                                    p_sc[bank + d][:],
                                                    qh[:, q0:q0 + 128],
                                                    kfull[hp][:, (t0 + d) * 512:
                                                               (t0 + d + 1) * 512],
                                                    start=True, stop=True)
                                            b.op("t", mmb,
                                                 reads=[("qh",), ("kfull", hp),
                                                        ("kfull_pe", hp)],
                                                 writes=[("p3s", bank + d)])
                                        ACT(lambda eng, t0=t0, s=s, par=par, pl=pl,
                                            bank=bank: eng.activation(
                                                out=attnb[:, t0 * 512:(t0 + 2) * 512],
                                                in_=p_sc4.ap()[:, bank * 512:
                                                               (bank + 2) * 512],
                                                func=AF.Exp, bias=biasq[par][:],
                                                scale=SCALE,
                                                accum_out=rs[:, pl, s:s + 1]),
                                            reads=[("biasq", par), ("p3s", bank),
                                                   ("p3s", bank + 1)],
                                            writes=[("attn", t0), ("attn", t0 + 1),
                                                    ("rs", pl)])
                                def trans_half(d, s=s):
                                    t0_ = chunk_t0[s - 1]
                                    td = t0_ + d
                                    base_pos = ((s - 1) * 8 + d * 4) % 16
                                    for j in range(4):
                                        def mmd(eng, td=td, j=j):
                                            kk = td * 4 + j
                                            return eng.transpose(
                                                p_dg[0][:, j * 128:(j + 1) * 128],
                                                attnb[:, kk * 128:(kk + 1) * 128],
                                                idb[:])
                                        b.op("t", mmd,
                                             reads=[("attn", td), ("idb",)],
                                             writes=[("p3d", 0)])
                                    slots = [base_pos + j for j in range(4)]
                                    if d == 0:
                                        DVE(lambda eng, s0=slots[0]: eng.tensor_copy(
                                                out=attnT[:, s0:s0 + 4, :],
                                                in_=p_dg[0][:].rearrange(
                                                    "p (j q) -> p j q", j=4)),
                                            reads=[("p3d", 0)],
                                            writes=[("attnT", s_) for s_ in slots])
                                    else:
                                        DVE(lambda eng, s0=slots[0]: eng.tensor_copy(
                                                out=attnT[:, s0:s0 + 4, :],
                                                in_=p_dg[0][:].rearrange(
                                                    "p (j q) -> p j q", j=4)),
                                            reads=[("p3d", 0)],
                                            writes=[("attnT", s_) for s_ in slots])
                                if 1 <= s <= m:
                                    trans_half(0)
                                if s >= 2:
                                    for x in range(8):
                                        pos = (s - 2) * 8 + x
                                        kj = kj_pos[pos]
                                        slot = pos % 16
                                        def mma(eng, kj=kj, slot=slot, pl=pl,
                                                hl=hl, pos=pos, nkj=nkj):
                                            return eng.matmul(
                                                p_av[:, pl * 128:(pl + 1) * 128],
                                                vbuf[:, kj, hl * 128:(hl + 1) * 128],
                                                attnT[:, slot, :],
                                                start=(pos == 0),
                                                stop=(pos == nkj - 1))
                                        b.op("t", mma,
                                             reads=[("vbuf",), ("attnT", slot)],
                                             writes=[("p3av",)])
                                if 1 <= s <= m:
                                    trans_half(1)
                                if last:
                                    for rop in grp_rowsums:
                                        rop()
                                    del grp_rowsums[:]
                                    w = len(grp) * 128
                                    DVE(lambda eng, w=w: eng.tensor_mul(
                                            avsb[:, 0:w], p_av[:, 0:w],
                                            rnorm[:, 0:w]),
                                        reads=[("p3av",), ("rnorm",)],
                                        writes=[("avsb",)])
                                    c0 = grp[0] * 128
                                    b.dma("q",
                                          attT_sc[h * 128:(h + 1) * 128,
                                                  c0:c0 + w],
                                          avsb[:, 0:w], reads=[("avsb",)],
                                          writes=[("attT_sc",)], sem="sc_at")
                            self.steps.append(step)
                        self.steps.reverse()

                    def step(self):
                        if self.steps:
                            self.steps.pop()()

                    def drain(self):
                        while self.steps:
                            self.steps.pop()()

                pend = [None]
                grp_rowsums = []

                def pstep():
                    if pend[0] is not None:
                        pend[0].step()
                for hg in range(H // 4):
                    h0 = 4 * hg
                    for hh in range(4):
                        b.dma("g", wkv[hg % 2][:, :, hh * VD:(hh + 1) * VD],
                              wkvbT_r[:, :, h0 + hh, NOPE:NOPE + VD],
                              reads=[("wkv_c", hg % 2)], writes=[("wkv", hg % 2)],
                              sem=f"wkv{hg % 2}")
                    for kj in range(NKT * 4):
                        for r in range(2):
                            def mmv(eng, kj=kj, r=r, hg=hg):
                                return eng.matmul(
                                    p_dec[kj % 2][:],
                                    ckv[r][:, kj * 128:(kj + 1) * 128],
                                    wkv[hg % 2][:, r, :],
                                    start=(r == 0), stop=(r == 1))
                            b.op("t", mmv, reads=[("ckvn",), ("wkv", hg % 2)],
                                 writes=[("p3k", kj % 2)])
                        ev_ = ACT if kj % 2 else DVE
                        ev_(lambda eng, kj=kj: (
                                eng.copy(out=vbuf[:, kj, :], in_=p_dec[kj % 2][:])
                                if hasattr(eng, "copy") and not hasattr(eng, "tensor_copy")
                                else eng.tensor_copy(out=vbuf[:, kj, :],
                                                     in_=p_dec[kj % 2][:])),
                            reads=[("p3k", kj % 2)],
                            writes=[("vbuf",)] +
                                   ([("wkv_c", hg % 2)] if kj == NKT * 4 - 1 else []))
                        pstep()
                    for hl in range(4):
                        h = h0 + hl
                        hp = h % 2
                        hq = h % 2
                        b.dma("g", wkn[hq][:],
                              wkvbT_r[:, :, h, 0:NOPE],
                              reads=[("wkn_c", hq)], writes=[("wkn", hq)],
                              sem=f"wkn{hq}")
                        b.dma("g", wqb_h[0][:],
                              wqbT.rearrange("c p o -> p c o")[:, :, h * 128:(h + 1) * 128],
                              reads=[("wqbh_c", 0)], writes=[("wqbh", 0)],
                              sem="wqb0")
                        for tt in range(NKT):
                            for r in range(2):
                                def mmk(eng, tt=tt, r=r, hq=hq):
                                    return eng.matmul(
                                        p_dec[tt % 2][64:128, :],
                                        wkn[hq][:, r, :],
                                        ckv[r][:, tt * 512:(tt + 1) * 512],
                                        start=(r == 0), stop=(r == 1),
                                        tile_position=(0, 64))
                                b.op("t", mmk, reads=[("ckvn",), ("wkn", hq)],
                                     writes=[("p3k", tt % 2)])
                            ACT(lambda eng, tt=tt, hp=hp: eng.copy(
                                    out=kfull[hp][64:128, tt * 512:(tt + 1) * 512],
                                    in_=p_dec[tt % 2][64:128, :]),
                                reads=[("p3k", tt % 2)],
                                writes=[("kfull", hp)] +
                                       ([("wkn_c", hq)] if tt == NKT - 1 else []))
                            pstep()
                        for tt in range(TT):
                            for c in range(NO8):
                                def mmq(eng, tt=tt, c=c):
                                    return eng.matmul(
                                        p_dec[tt % 2][:],
                                        wqb_h[0][:, c, :],
                                        qan[:, c, tt * 512:(tt + 1) * 512],
                                        start=(c == 0), stop=(c == NO8 - 1))
                                b.op("t", mmq, reads=[("qan",), ("wqbh", 0)],
                                     writes=[("p3k", tt % 2)])
                            ACT(lambda eng, tt=tt: eng.copy(
                                    out=qpe[:, tt * 512:(tt + 1) * 512],
                                    in_=p_dec[tt % 2][0:ROPE, :]),
                                reads=[("p3k", tt % 2)],
                                writes=[("qpe", tt)])
                            ACT(lambda eng, tt=tt: eng.copy(
                                    out=qh[ROPE:128, tt * 512:(tt + 1) * 512],
                                    in_=p_dec[tt % 2][ROPE:128, :]),
                                reads=[("p3k", tt % 2)],
                                writes=[("qh",), ("wqbh_c", 0)])
                            pstep()
                        for tt in range(TT):
                            for pi, dbuf in enumerate((d1b, d2b)):
                                def mmr(eng, pi=pi, dbuf=dbuf, tt=tt):
                                    return eng.matmul(
                                        p_dec[pi][0:ROPE, :], dbuf[:],
                                        qpe[:, tt * 512:(tt + 1) * 512],
                                        start=True, stop=True)
                                b.op("t", mmr,
                                     reads=[("qpe", tt), ("d1b",), ("d2b",)],
                                     writes=[("p3k", pi)])
                            DVE(lambda eng, tt=tt: eng.tensor_mul(
                                    tmpq[:, 0:512],
                                    p_dec[1][0:ROPE, :],
                                    ropeq[:, 1, tt * 512:(tt + 1) * 512]),
                                reads=[("p3k", 1), ("ropeq",)], writes=[("tmpq", tt)])
                            DVE(lambda eng, tt=tt: eng.tensor_mul(
                                    p_dec[0][0:ROPE, :],
                                    p_dec[0][0:ROPE, :],
                                    ropeq[:, 0, tt * 512:(tt + 1) * 512]),
                                reads=[("p3k", 0), ("ropeq",)], writes=[("p3k", 0)])
                            DVE(lambda eng, tt=tt: eng.tensor_add(
                                    qh[0:ROPE, tt * 512:(tt + 1) * 512],
                                    p_dec[0][0:ROPE, :],
                                    tmpq[:, 0:512]),
                                reads=[("tmpq", tt), ("p3k", 0)], writes=[("qh",)])
                            pstep()
                        for grp in pgroups:
                            for pl, pz in enumerate(grp):
                                m = pz + 1
                                q0 = (2 * m - 2) * 64
                                for sl_i, kt in enumerate((2 * m - 2, 2 * m - 1)):
                                    def mms(eng, kt=kt, q0=q0, hp=hp, sl_i=sl_i):
                                        return eng.matmul(
                                            p_dec[sl_i][:],
                                            qh[:, q0:q0 + 128],
                                            kfull[hp][:, kt * 512:(kt + 1) * 512],
                                            start=True, stop=True)
                                    b.op("t", mms,
                                         reads=[("qh",), ("kfull", hp),
                                                ("kfull_pe", hp)],
                                         writes=[("p3k", sl_i)])
                                    msk = maskA_sb if sl_i == 0 else maskB_sb
                                    mk = ("maskA",) if sl_i == 0 else ("maskB",)
                                    DVE(lambda eng, sl_i=sl_i, msk=msk:
                                        eng.scalar_tensor_tensor(
                                            out=p_dec[sl_i][:],
                                            in0=p_dec[sl_i][:], scalar=1.0,
                                            in1=msk[:],
                                            op0=ALU.mult, op1=ALU.add),
                                        reads=[("p3k", sl_i), mk],
                                        writes=[("p3k", sl_i)])
                                    DVE(lambda eng, sl_i=sl_i: eng.reduce_max(
                                            mx[:, sl_i:sl_i + 1],
                                            p_dec[sl_i][:],
                                            axis=AX),
                                        reads=[("p3k", sl_i)],
                                        writes=[("mx",)])
                                    pstep()
                                if pend[0] is not None:
                                    pend[0].drain()
                                pend[0] = Pending(h, hl, hp, m, pl,
                                                  pz == grp[-1], grp, 0)
                if pend[0] is not None:
                    pend[0].drain()
            b.barrier()

        with ExitStack() as eB:
            x1buf = sb(eB, "x1buf", (128, KCH, T), BF16)
            esl = [sb(eB, f"esl{i}", (NE, 128), BF16) for i in range(2)]
            gw_sb = sb(eB, "gw_sb", (128, KCH, NE), BF16)
            id16b = sb(eB, "id16b", (16, 16), F32)
            id128fb = sb(eB, "id128fb", (128, 128), F32)
            b.dma("g", gw_sb[:], gwT.rearrange("c p o -> p c o"), writes=[("gw",)])
            b.dma("q", id16b[:], id16f_c, writes=[("id16",)])
            b.dma("q", id128fb[:], id128f_c, writes=[("id128f",)])

            with ExitStack() as e4:
                stat_m = sb(e4, "stat_m", (1, T), F32)
                stat_v = sb(e4, "stat_v", (1, T), F32)
                stat_l = sb(e4, "stat_l", (1, T), F32)
                rstd4 = sb(e4, "rstd4", (1, T), BF16)
                nmrs4 = sb(e4, "nmrs4", (1, T), BF16)
                Asb = sb(e4, "Asb", (128, 2, 512), F32)
                Bsb = sb(e4, "Bsb", (128, 2, 512), F32)
                attT = sb(e4, "attT", (128, KCH, T), BF16)
                woStrip = [sb(e4, f"woS{i}", (128, D), BF16) for i in range(2)]
                xres = [sb(e4, f"xres{i}", (128, T), F32) for i in range(2)]
                sq4 = sb(e4, "sq4", (128, T), BF16)
                p_wo = [psb(e4, f"p4w{i}", (128, 512)) for i in range(2)]
                p_sx = [psb(e4, f"p4sx{i}", (1, 512)) for i in range(TT)]
                p_sq = [psb(e4, f"p4sq{i}", (1, 512)) for i in range(TT)]
                p_bc4 = [psb(e4, f"p4bc{i}", (128, 512)) for i in range(2)]

                for c in range(KCH):
                    b.dma("q", attT[:, c, :], attT_sc[c * 128:(c + 1) * 128, :],
                          reads=[("attT_sc",)], writes=[("attT", c)],
                          sem=f"at{c % 4}")
                for o in range(KCH):
                    os2 = o % 2
                    b.dma("g", woStrip[os2][:], woTs[o * 128:(o + 1) * 128, :],
                          reads=[("woS_c", os2)], writes=[("woS", os2)],
                          sem=f"woS{os2}")
                    b.dma("q", xres[os2][:], xqT[o * 128:(o + 1) * 128, :],
                          reads=[("xres_c", os2)], writes=[("xres", os2)],
                          sem=f"xr{os2}")
                    for tt in range(TT):
                        for c in range(KCH):
                            def mmw(eng, os2=os2, tt=tt, c=c):
                                return eng.matmul(
                                    p_wo[os2][:],
                                    woStrip[os2][:, c * 128:(c + 1) * 128],
                                    attT[:, c, tt * 512:(tt + 1) * 512],
                                    start=(c == 0), stop=(c == KCH - 1))
                            b.op("t", mmw, reads=[("woS", os2), ("attT", c)],
                                 writes=[("p4w", os2)])
                        DVE(lambda eng, os2=os2, tt=tt, o=o: eng.tensor_add(
                                x1buf[:, o, tt * 512:(tt + 1) * 512],
                                p_wo[os2][:],
                                xres[os2][:, tt * 512:(tt + 1) * 512]),
                            reads=[("p4w", os2), ("xres", os2)],
                            writes=[("x1pre", o, tt), ("xres_c", os2)])
                        ACT(lambda eng, o=o, tt=tt: eng.square(
                                out=sq4[:, tt * 512:(tt + 1) * 512],
                                in_=x1buf[:, o, tt * 512:(tt + 1) * 512]),
                            reads=[("x1pre", o, tt)], writes=[("sq4", tt)])
                        def mmsx(eng, o=o, tt=tt):
                            return eng.matmul(
                                p_sx[tt][:], onesb[:],
                                x1buf[:, o, tt * 512:(tt + 1) * 512],
                                start=(o == 0), stop=(o == KCH - 1))
                        b.op("t", mmsx, reads=[("x1pre", o, tt), ("onesb",)],
                             writes=[("p4sx", tt)])
                        def mmsq(eng, o=o, tt=tt):
                            return eng.matmul(p_sq[tt][:], onesb[:],
                                              sq4[:, tt * 512:(tt + 1) * 512],
                                              start=(o == 0), stop=(o == KCH - 1))
                        b.op("t", mmsq, reads=[("sq4", tt), ("onesb",)],
                             writes=[("p4sq", tt)])
                for tt in range(TT):
                    sl = slice(tt * 512, (tt + 1) * 512)
                    ACT(lambda eng, sl=sl, tt=tt: eng.activation(
                            out=stat_m[:, sl], in_=p_sx[tt][:], func=AF.Copy,
                            scale=1.0 / D),
                        reads=[("p4sx", tt)], writes=[("st4m", tt)])
                    ACT(lambda eng, sl=sl, tt=tt: eng.activation(
                            out=stat_v[:, sl], in_=p_sq[tt][:], func=AF.Copy,
                            scale=1.0 / D),
                        reads=[("p4sq", tt)], writes=[("st4v", tt)])
                    ACT(lambda eng, sl=sl, tt=tt: eng.square(
                            out=stat_l[:, sl], in_=stat_m[:, sl]),
                        reads=[("st4m", tt)], writes=[("st4l", tt)])
                    DVE(lambda eng, sl=sl: eng.tensor_sub(
                            stat_v[:, sl], stat_v[:, sl], stat_l[:, sl]),
                        reads=[("st4v", tt), ("st4l", tt)], writes=[("st4v", tt)])
                    ACT(lambda eng, sl=sl, tt=tt: eng.activation(
                            out=stat_l[:, sl], in_=stat_v[:, sl], func=AF.Ln,
                            bias=eps_t[0:1, 1:2]),
                        reads=[("st4v", tt), ("eps",)], writes=[("st4l", tt)])
                    ACT(lambda eng, sl=sl, tt=tt: eng.activation(
                            out=rstd4[:, sl], in_=stat_l[:, sl], func=AF.Exp,
                            scale=-0.5),
                        reads=[("st4l", tt)], writes=[("rstd4", tt)])
                    DVE(lambda eng, sl=sl: eng.tensor_mul(
                            stat_m[:, sl], stat_m[:, sl], rstd4[:, sl]),
                        reads=[("st4m", tt), ("rstd4", tt)], writes=[("st4m", tt)])
                    DVE(lambda eng, sl=sl: eng.tensor_scalar_mul(
                            nmrs4[:, sl], stat_m[:, sl], -1.0),
                        reads=[("st4m", tt)], writes=[("nmrs4", tt)])
                    def mba(eng, sl=sl):
                        return eng.matmul(p_bc4[0][:], onesk1b[:], rstd4[:, sl],
                                          start=True, stop=True)
                    b.op("t", mba, reads=[("rstd4", tt), ("onesk1b",)],
                         writes=[("p4bcA",)])
                    def mbb(eng, sl=sl):
                        return eng.matmul(p_bc4[1][:], onesk1b[:], nmrs4[:, sl],
                                          start=True, stop=True)
                    b.op("t", mbb, reads=[("nmrs4", tt), ("onesk1b",)],
                         writes=[("p4bcB",)])
                    ACT(lambda eng, tt=tt: eng.copy(out=Asb[:, tt, :],
                                                    in_=p_bc4[0][:]),
                        reads=[("p4bcA",)], writes=[("Asb", tt)])
                    ACT(lambda eng, tt=tt: eng.copy(out=Bsb[:, tt, :],
                                                    in_=p_bc4[1][:]),
                        reads=[("p4bcB",)], writes=[("Bsb", tt)])
                    for c in range(KCH):
                        DVE(lambda eng, c=c, tt=tt: eng.tensor_mul(
                                x1buf[:, c, tt * 512:(tt + 1) * 512],
                                x1buf[:, c, tt * 512:(tt + 1) * 512],
                                Asb[:, tt, :]),
                            reads=[("x1pre", c, tt), ("Asb", tt)],
                            writes=[("x1pre", c, tt)])
                        DVE(lambda eng, c=c, tt=tt: eng.tensor_add(
                                x1buf[:, c, tt * 512:(tt + 1) * 512],
                                x1buf[:, c, tt * 512:(tt + 1) * 512],
                                Bsb[:, tt, :]),
                            reads=[("x1pre", c, tt), ("Bsb", tt)],
                            writes=[("x1", c, tt)])
            b.barrier()

            for th in range(TT):
                tsl = slice(th * 512, (th + 1) * 512)
                with ExitStack() as eH:
                    hw = sb(eH, "hw", (128, NIC, 512), BF16)
                    with ExitStack() as e5:
                        lgT = sb(e5, "lgT", (16, 512), F32)
                        lgtm = sb(e5, "lgtm", (128, 4, 16), F32)
                        lgr = sb(e5, "lgr", (128, 4, 16), F32)
                        m1 = sb(e5, "m1", (128, 4), F32)
                        m2 = sb(e5, "m2", (128, 4), F32)
                        w1 = sb(e5, "w1", (128, 4), F32)
                        w2 = sb(e5, "w2", (128, 4), F32)
                        msk1 = sb(e5, "msk1", (128, 4, 16), F32)
                        msk2 = sb(e5, "msk2", (128, 4, 16), F32)
                        wgt = sb(e5, "wgt", (128, 4, 16), F32)
                        wTf = sb(e5, "wTf", (16, 512), F32)
                        wTb = sb(e5, "wTb", (16, 512), BF16)
                        uS = [sb(e5, f"uS{i}", (128, D), BF16) for i in range(2)]
                        gS = [sb(e5, f"gS{i}", (128, D), BF16) for i in range(2)]
                        Wm = sb(e5, "Wm", (128, 512), BF16)
                        sg_sb = sb(e5, "sg_sb", (128, 512), BF16)
                        ht_sb = sb(e5, "ht_sb", (128, 512), BF16)
                        p_u = [psb(e5, f"p5u{i}", (128, 512)) for i in range(2)]
                        p_g = [psb(e5, f"p5g{i}", (128, 512)) for i in range(2)]
                        p_wm = psb(e5, "p5wm", (128, 512))
                        p_gt = psb(e5, "p5gt", (128, 512))

                        for c in range(KCH):
                            def mmg(eng, c=c, tsl=tsl):
                                return eng.matmul(p_gt[0:16, :], gw_sb[:, c, :],
                                                  x1buf[:, c, tsl],
                                                  start=(c == 0),
                                                  stop=(c == KCH - 1))
                            b.op("t", mmg, reads=[("x1", c, th), ("gw",)],
                                 writes=[("p5gt",)])
                        DVE(lambda eng: eng.tensor_copy(out=lgT[:],
                                                        in_=p_gt[0:16, :]),
                            reads=[("p5gt",)], writes=[("lgT",)])
                        for sub in range(4):
                            def mtr(eng, sub=sub):
                                return eng.transpose(
                                    p_gt[:, 0:16],
                                    lgT[:, sub * 128:(sub + 1) * 128],
                                    id16b[:])
                            b.op("t", mtr, reads=[("lgT",), ("id16",)],
                                 writes=[("p5gt",)])
                            DVE(lambda eng, sub=sub: eng.tensor_copy(
                                    out=lgtm[:, sub, :], in_=p_gt[:, 0:16]),
                                reads=[("p5gt",)], writes=[("lgtm", sub)])
                        rall = [("lgtm", s) for s in range(4)]
                        DVE(lambda eng: eng.reduce_max(m1[:], lgtm.ap(), axis=AX),
                            reads=rall, writes=[("m1",)])
                        for sub in range(4):
                            DVE(lambda eng, sub=sub: eng.tensor_single_scalar(
                                    out=msk1[:, sub, :], in_=lgtm[:, sub, :],
                                    scalar=m1[:, sub:sub + 1], op=ALU.is_ge),
                                reads=rall + [("m1",)], writes=[("msk1",)])
                        DVE(lambda eng: eng.scalar_tensor_tensor(
                                out=lgr.ap(), in0=msk1.ap(), scalar=NEG,
                                in1=lgtm.ap(), op0=ALU.mult, op1=ALU.add),
                            reads=[("msk1",)] + rall, writes=[("lgr",)])
                        DVE(lambda eng: eng.reduce_max(m2[:], lgr.ap(), axis=AX),
                            reads=[("lgr",)], writes=[("m2",)])
                        for sub in range(4):
                            ACT(lambda eng, sub=sub: eng.activation(
                                    out=w1[:, sub:sub + 1], in_=m2[:, sub:sub + 1],
                                    func=AF.Sigmoid, bias=m1[:, sub:sub + 1],
                                    scale=-1.0),
                                reads=[("m1",), ("m2",)], writes=[("w1",)])
                        DVE(lambda eng: eng.tensor_scalar(
                                out=w2[:], in0=w1[:], scalar1=-1.0, scalar2=1.0,
                                op0=ALU.mult, op1=ALU.add),
                            reads=[("w1",)], writes=[("w2",)])
                        for sub in range(4):
                            DVE(lambda eng, sub=sub: eng.tensor_single_scalar(
                                    out=msk2[:, sub, :], in_=lgr[:, sub, :],
                                    scalar=m2[:, sub:sub + 1], op=ALU.is_ge),
                                reads=[("lgr",), ("m2",)], writes=[("msk2",)])
                            DVE(lambda eng, sub=sub: eng.tensor_scalar_mul(
                                    msk1[:, sub, :], msk1[:, sub, :],
                                    w1[:, sub:sub + 1]),
                                reads=[("msk1",), ("w1",)], writes=[("msk1",)])
                            DVE(lambda eng, sub=sub: eng.tensor_scalar_mul(
                                    msk2[:, sub, :], msk2[:, sub, :],
                                    w2[:, sub:sub + 1]),
                                reads=[("msk2",), ("w2",)], writes=[("msk2",)])
                        DVE(lambda eng: eng.tensor_add(wgt.ap(), msk1.ap(),
                                                       msk2.ap()),
                            reads=[("msk1",), ("msk2",)], writes=[("wgt",)])
                        for sub in range(4):
                            def mtr2(eng, sub=sub):
                                return eng.transpose(p_gt[0:16, 0:128],
                                                     wgt[:, sub, :],
                                                     id128fb[:])
                            b.op("t", mtr2, reads=[("wgt",), ("id128f",)],
                                 writes=[("p5gt",)])
                            DVE(lambda eng, sub=sub: eng.tensor_copy(
                                    out=wTf[:, sub * 128:(sub + 1) * 128],
                                    in_=p_gt[0:16, 0:128]),
                                reads=[("p5gt",)], writes=[("wTf",)])
                        DVE(lambda eng: eng.tensor_copy(out=wTb[:], in_=wTf[:]),
                            reads=[("wTf",)], writes=[("wTb",)])
                        for ot in range(NIC):
                            ot2 = ot % 2
                            b.dma("g", uS[ot2][:], uTs[ot * 128:(ot + 1) * 128, :],
                                  reads=[("uS_c", ot2)], writes=[("uS", ot2)],
                                  sem=f"uS{ot2}")
                            b.dma("g", gS[ot2][:], gTs[ot * 128:(ot + 1) * 128, :],
                                  reads=[("gS_c", ot2)], writes=[("gS", ot2)],
                                  sem=f"gS{ot2}")
                            for c in range(KCH):
                                def mmu(eng, ot2=ot2, c=c, tsl=tsl):
                                    return eng.matmul(
                                        p_u[ot2][:],
                                        uS[ot2][:, c * 128:(c + 1) * 128],
                                        x1buf[:, c, tsl],
                                        start=(c == 0), stop=(c == KCH - 1))
                                b.op("t", mmu, reads=[("uS", ot2), ("x1", c, th)],
                                     writes=[("p5u", ot2)])
                                def mmg2(eng, ot2=ot2, c=c, tsl=tsl):
                                    return eng.matmul(
                                        p_g[ot2][:],
                                        gS[ot2][:, c * 128:(c + 1) * 128],
                                        x1buf[:, c, tsl],
                                        start=(c == 0), stop=(c == KCH - 1))
                                b.op("t", mmg2, reads=[("gS", ot2), ("x1", c, th)],
                                     writes=[("p5g", ot2)])
                            b.dma("q", esl[ot2][:],
                                  esel_c[:, ot * 128:(ot + 1) * 128],
                                  reads=[("esl_c", ot2)], writes=[("esl", ot2)],
                                  sem=f"esl{ot2}")
                            def mmwm(eng, ot2=ot2):
                                return eng.matmul(p_wm[:], esl[ot2][:],
                                                  wTb[:], start=True, stop=True)
                            b.op("t", mmwm, reads=[("esl", ot2), ("wTb",)],
                                 writes=[("p5wm",), ("esl_c", ot2)])
                            ACT(lambda eng: eng.copy(out=Wm[:], in_=p_wm[:]),
                                reads=[("p5wm",)], writes=[("Wm",)])
                            ACT(lambda eng, ot2=ot2: eng.activation(
                                    out=sg_sb[:], in_=p_g[ot2][:], func=AF.Silu),
                                reads=[("p5g", ot2)], writes=[("sg",), ("gS_c", ot2)])
                            DVE(lambda eng, ot2=ot2: eng.tensor_mul(
                                    ht_sb[:], p_u[ot2][:], sg_sb[:]),
                                reads=[("p5u", ot2), ("sg",)],
                                writes=[("ht",), ("uS_c", ot2)])
                            DVE(lambda eng, ot=ot: eng.tensor_mul(
                                    hw[:, ot, :], ht_sb[:], Wm[:]),
                                reads=[("ht",), ("Wm",)], writes=[("hw", ot)])
                    b.barrier()
                    with ExitStack() as e6:
                        dS = [sb(e6, f"dS{i}", (128, DI), BF16) for i in range(2)]
                        x2pre = sb(e6, "x2pre", (128, KCH, 512), BF16)
                        sq6 = sb(e6, "sq6", (128, 512), BF16)
                        st6_m = sb(e6, "st6_m", (1, 512), F32)
                        st6_v = sb(e6, "st6_v", (1, 512), F32)
                        st6_l = sb(e6, "st6_l", (1, 512), F32)
                        rstd6 = sb(e6, "rstd6", (1, 512), BF16)
                        nmrs6 = sb(e6, "nmrs6", (1, 512), BF16)
                        A6 = sb(e6, "A6", (128, 512), BF16)
                        B6 = sb(e6, "B6", (128, 512), BF16)
                        outsb = sb(e6, "outsb", (128, 512), F32)
                        p_d = [psb(e6, f"p6d{i}", (128, 512)) for i in range(2)]
                        p_s6 = [psb(e6, f"p6s{i}", (1, 512)) for i in range(2)]
                        p_b6 = psb(e6, "p6b", (128, 512))

                        for o in range(KCH):
                            os2 = o % 2
                            b.dma("g", dS[os2][:], dTs[o * 128:(o + 1) * 128, :],
                                  reads=[("dS_c", os2)], writes=[("dS", os2)],
                                  sem=f"dS{os2}")
                            for ic in range(NIC):
                                def mmd2(eng, os2=os2, ic=ic):
                                    return eng.matmul(
                                        p_d[os2][:],
                                        dS[os2][:, ic * 128:(ic + 1) * 128],
                                        hw[:, ic, :],
                                        start=(ic == 0), stop=(ic == NIC - 1))
                                b.op("t", mmd2, reads=[("dS", os2), ("hw", ic)],
                                     writes=[("p6d", os2)])
                            DVE(lambda eng, os2=os2, o=o, tsl=tsl: eng.tensor_add(
                                    x2pre[:, o, :], p_d[os2][:],
                                    x1buf[:, o, tsl]),
                                reads=[("p6d", os2), ("x1", o, th)],
                                writes=[("x2pre", o), ("dS_c", os2)])
                            ACT(lambda eng, o=o: eng.square(
                                    out=sq6[:], in_=x2pre[:, o, :]),
                                reads=[("x2pre", o)], writes=[("sq6",)])
                            def mmsx6(eng, o=o):
                                return eng.matmul(p_s6[0][:], onesb[:],
                                                  x2pre[:, o, :],
                                                  start=(o == 0),
                                                  stop=(o == KCH - 1))
                            b.op("t", mmsx6, reads=[("x2pre", o), ("onesb",)],
                                 writes=[("p6sx",)])
                            def mmsq6(eng, o=o):
                                return eng.matmul(p_s6[1][:], onesb[:], sq6[:],
                                                  start=(o == 0),
                                                  stop=(o == KCH - 1))
                            b.op("t", mmsq6, reads=[("sq6",), ("onesb",)],
                                 writes=[("p6sq",)])
                        ACT(lambda eng: eng.activation(out=st6_m[:], in_=p_s6[0][:],
                                                       func=AF.Copy, scale=1.0 / D),
                            reads=[("p6sx",)], writes=[("st6m",)])
                        ACT(lambda eng: eng.activation(out=st6_v[:], in_=p_s6[1][:],
                                                       func=AF.Copy, scale=1.0 / D),
                            reads=[("p6sq",)], writes=[("st6v",)])
                        ACT(lambda eng: eng.square(out=st6_l[:], in_=st6_m[:]),
                            reads=[("st6m",)], writes=[("st6l",)])
                        DVE(lambda eng: eng.tensor_sub(st6_v[:], st6_v[:],
                                                       st6_l[:]),
                            reads=[("st6v",), ("st6l",)], writes=[("st6v",)])
                        ACT(lambda eng: eng.activation(out=st6_l[:], in_=st6_v[:],
                                                       func=AF.Ln,
                                                       bias=eps_t[0:1, 1:2]),
                            reads=[("st6v",), ("eps",)], writes=[("st6l",)])
                        ACT(lambda eng: eng.activation(out=rstd6[:], in_=st6_l[:],
                                                       func=AF.Exp, scale=-0.5),
                            reads=[("st6l",)], writes=[("rstd6",)])
                        DVE(lambda eng: eng.tensor_mul(st6_m[:], st6_m[:],
                                                       rstd6[:]),
                            reads=[("st6m",), ("rstd6",)], writes=[("st6m",)])
                        DVE(lambda eng: eng.tensor_scalar_mul(nmrs6[:], st6_m[:],
                                                              -1.0),
                            reads=[("st6m",)], writes=[("nmrs6",)])
                        def mb6a(eng):
                            return eng.matmul(p_b6[:], onesk1b[:], rstd6[:],
                                              start=True, stop=True)
                        b.op("t", mb6a, reads=[("rstd6",), ("onesk1b",)],
                             writes=[("p6bA",)])
                        ACT(lambda eng: eng.copy(out=A6[:], in_=p_b6[:]),
                            reads=[("p6bA",)], writes=[("A6",)])
                        def mb6b(eng):
                            return eng.matmul(p_b6[:], onesk1b[:], nmrs6[:],
                                              start=True, stop=True)
                        b.op("t", mb6b, reads=[("nmrs6",), ("onesk1b",), ("A6",)],
                             writes=[("p6bB",)])
                        ACT(lambda eng: eng.copy(out=B6[:], in_=p_b6[:]),
                            reads=[("p6bB",)], writes=[("B6",)])
                        for o in range(KCH):
                            DVE(lambda eng, o=o: eng.tensor_mul(
                                    outsb[:], x2pre[:, o, :], A6[:]),
                                reads=[("x2pre", o), ("A6",)], writes=[("outsb",)])
                            DVE(lambda eng: eng.tensor_add(
                                    outsb[:], outsb[:], B6[:]),
                                reads=[("outsb",), ("B6",)], writes=[("outsb",)])
                            b.dma("q", outT[o * 128:(o + 1) * 128, tsl], outsb[:],
                                  reads=[("outsb",)], writes=[("outT",)],
                                  sem="out")
                b.barrier()

        with nc.Block() as block:
            b.finalize_and_emit(block, sems)
    return nc


def stage_inputs(inputs, S):
    T = S // NCORES
    NSUB = T // 64
    x = np.asarray(inputs["L0_attn_x"], np.float32).reshape(S, D)
    wq_a = np.asarray(inputs["L0_attn_wq_a"], np.float32)
    q_ln = np.asarray(inputs["L0_attn_q_a_ln_w"], np.float32)
    wq_b = np.asarray(inputs["L0_attn_wq_b"], np.float32)
    wkv_a = np.asarray(inputs["L0_attn_wkv_a"], np.float32)
    kv_ln = np.asarray(inputs["L0_attn_kv_a_ln_w"], np.float32)
    wkv_b = np.asarray(inputs["L0_attn_wkv_b"], np.float32)
    wo = np.asarray(inputs["L0_attn_wo"], np.float32)
    gate_w = np.asarray(inputs["L0_ffn_gate_w"], np.float32)
    u = np.asarray(inputs["L0_ffn_u"], np.float32)
    g = np.asarray(inputs["L0_ffn_g"], np.float32)
    d = np.asarray(inputs["L0_ffn_d"], np.float32)

    xT = np.ascontiguousarray(x.T)
    wqaT = np.ascontiguousarray(
        wq_a.T.reshape(KCH, 128, DQA // 128, 128).transpose(2, 1, 0, 3)).reshape(
        DQA // 128 * 128, D)
    wqb = wq_b.T.reshape(DQA, H, QHD)
    wqb2 = np.concatenate([wqb[:, :, NOPE:], wqb[:, :, :NOPE]], axis=2)
    wqbT = np.ascontiguousarray(wqb2.reshape(DQA, D)).reshape(DQA // 128, 128, D)
    wkvaT = np.ascontiguousarray(wkv_a.T).reshape(KCH, 128, KVR + ROPE)
    wkvbT = np.ascontiguousarray(wkv_b.T).reshape(KVR // 128, 128, H * (NOPE + VD))
    woTs = np.ascontiguousarray(
        wo.T.reshape(KCH, 128, KCH, 128).transpose(2, 1, 0, 3)).reshape(KCH * 128, D)
    uTs = np.ascontiguousarray(
        u.T.reshape(KCH, 128, NIC, 128).transpose(2, 1, 0, 3)).reshape(NIC * 128, D)
    gTs = np.ascontiguousarray(
        g.T.reshape(KCH, 128, NIC, 128).transpose(2, 1, 0, 3)).reshape(NIC * 128, D)
    dTs = np.ascontiguousarray(
        d.T.reshape(NIC, 128, KCH, 128).transpose(2, 1, 0, 3)).reshape(KCH * 128, DI)
    gwT = np.ascontiguousarray(gate_w.T).reshape(KCH, 128, NE)
    qlnw = np.ascontiguousarray(q_ln.reshape(DQA // 128, 128).T)
    kvlnw = np.ascontiguousarray(kv_ln.reshape(KVR // 128, 128).T)
    cosf, sinf = yarn_cos_sin(S)
    coskT = np.ascontiguousarray(cosf.T)
    sinkT = np.ascontiguousarray(sinf.T)

    shared = dict(xT=xT, wqaT=wqaT, wqbT=wqbT, wkvaT=wkvaT, wkvbT=wkvbT,
                  woTs=woTs, uTs=uTs, gTs=gTs, dTs=dTs, gwT=gwT,
                  qlnw=qlnw, kvlnw=kvlnw, coskT=coskT, sinkT=sinkT)
    in_maps = []
    tok_idx_all = []
    for c in range(NCORES):
        toks = np.concatenate([np.arange(64) + (8 * j + c) * 64
                               for j in range(NSUB)])
        tok_idx_all.append(toks)
        r = np.arange(128) % 64
        bound = 64 * c + r
        kk = np.arange(512)[None, :]
        mA = np.where(kk <= bound[:, None], 0.0, NEG).astype(np.float32)
        mA[64:, :] = 0.0
        mB = np.where(kk <= bound[:, None], 0.0, NEG).astype(np.float32)
        mB[:64, :] = NEG
        mA = mA.astype(BF)
        mB = mB.astype(BF)
        m = dict(shared)
        m["xqT"] = np.ascontiguousarray(xT[:, toks])
        m["cosqT"] = np.ascontiguousarray(coskT[:, toks])
        m["sinqT"] = np.ascontiguousarray(sinkT[:, toks])
        m["maskA"] = mA
        m["maskB"] = mB
        in_maps.append(m)
    return in_maps, tok_idx_all


def kernel(**inputs):
    x = np.asarray(inputs["L0_attn_x"])
    S = x.shape[1]
    nc = build_program(S)
    in_maps, tok_idx = stage_inputs(inputs, S)
    res = run_bass_kernel_spmd(nc, in_maps, core_ids=list(range(NCORES)))
    out = np.empty((S, D), np.float32)
    for c in range(NCORES):
        out[tok_idx[c], :] = np.asarray(res.results[c]["outT"]).T
    return out.reshape(1, S, D)
